# version 11
# baseline (speedup 1.0000x reference)
import numpy as np
import concourse.bass as bass
import concourse.mybir as mybir
from concourse.bass import IndirectOffsetOnAxis
from concourse.bass_utils import run_bass_kernel_spmd

F32 = mybir.dt.float32
BF16 = mybir.dt.bfloat16
I32 = mybir.dt.int32
AF = mybir.ActivationFunctionType
ALU = mybir.AluOpType
AX = mybir.AxisListType

B, S, D = 8, 8192, 1024
T = 1024
NB = S // T
NTT = T // 128
NDT = D // 128
EX = 512
NET = EX // 128
NRE = 16
NEXP = 18
NCH = 32
NSLOT = NCH * 128

CB_RB = 0
CB_BL = 16
CB_RW = 48
CB_EPS_LN = 1072
CB_EPS_RMS = 1073
CB_MHALF = 1074
CB_INVD = 1075
CB_IOTA = 1076
CB_RB8 = 1080
CB_BL8 = 1208
CB_C128 = 1464
CB_ONES16 = 1496
CB_W = 1512


class Sem:
    def __init__(self, nc, name, owner, step=1):
        self.name, self.owner, self.step = name, owner, step
        self.h = nc.alloc_semaphore(name)
        self.n = 0
        self.val = {}

    def record(self, ev):
        assert ev not in self.val, f"dup event {self.name}:{ev}"
        self.n += 1
        self.val[ev] = self.n * self.step

    def alias(self, new, old):
        self.val[new] = self.val[old]


class Px:

    def __init__(self, mode, eng=None):
        self.mode = mode
        self.eng = eng
        self.hwm = {}

    def op(self, engine, waits, fn, inc=None, fwaits=()):
        if self.mode is None:
            if inc is not None:
                inc[0].record(inc[1])
            return
        if self.mode != engine:
            return
        e = self.eng
        for sem, ev in waits:
            if sem.owner == engine:
                continue
            tgt = sem.val[ev]
            if self.hwm.get(sem.name, 0) >= tgt:
                continue
            e.wait_ge(sem.h, tgt)
            self.hwm[sem.name] = tgt
        for sem, ev in fwaits:
            tgt = sem.val[ev]
            if self.hwm.get(sem.name, 0) >= tgt:
                continue
            e.wait_ge(sem.h, tgt)
            self.hwm[sem.name] = tgt
        last = fn(e)
        if inc is not None:
            sem, ev = inc
            assert last is not None, f"inc {ev} with no instruction"
            last.then_inc(sem.h, sem.step)

    def alias(self, sem, new, old):
        if self.mode is None:
            sem.alias(new, old)


class NS:
    pass


def build_tensors(nc):
    t = NS()
    dt_ = nc.dram_tensor
    t.x_d = dt_("x", [S, D], F32, kind="ExternalInput").ap()
    t.y_d = dt_("y", [S, D], F32, kind="ExternalOutput").ap()
    t.up_d = dt_("up", [NRE, D, EX], F32, kind="ExternalInput").ap()
    t.gate_d = dt_("gate", [NRE, D, EX], F32, kind="ExternalInput").ap()
    t.down_d = dt_("down", [NRE, EX, D], F32, kind="ExternalInput").ap()
    t.shu_t = dt_("shu_t", [D, D], F32, kind="ExternalInput").ap()
    t.shg_t = dt_("shg_t", [D, D], F32, kind="ExternalInput").ap()
    t.shd_t = dt_("shd_t", [D, D], F32, kind="ExternalInput").ap()
    t.wgate_t = dt_("wgate_t", [D, D], F32, kind="ExternalInput").ap()
    t.router_t = dt_("router_t", [D, 16], F32, kind="ExternalInput").ap()
    t.wlamb_t = dt_("wlamb_t", [D, 32], F32, kind="ExternalInput").ap()
    t.wct_d = dt_("wct", [16, D], F32, kind="ExternalInput").ap()
    t.cbias_d = dt_("cbias", [128, CB_W], F32, kind="ExternalInput").ap()
    t.identb_d = dt_("identb", [128, 128], BF16, kind="ExternalInput").ap()
    t.identf_d = dt_("identf", [128, 128], F32, kind="ExternalInput").ap()
    t.toki_d = dt_("toki", [128, NTT], I32, kind="ExternalInput").ap()
    t.wexp2 = dt_("wexp2", [NEXP, 128, NDT * 1024], BF16, kind="Internal").ap()
    t.wdn2 = dt_("wdn2", [NEXP, 128, NET * 1024], BF16, kind="Internal").ap()
    t.wgate2 = dt_("wgate2", [128, NDT * 1024], BF16, kind="Internal").ap()
    t.xlnb_d = dt_("xlnb_d", [T, D], BF16, kind="Internal").ap()
    t.tokmap_d = dt_("tokmap_d", [NSLOT, 1], I32, kind="Internal").ap()
    t.ws_d = dt_("ws_d", [NSLOT, 1], F32, kind="Internal").ap()
    t.out_d = dt_("out_d", [NSLOT, D], F32, kind="Internal").ap()

    sb = nc.alloc_sbuf_tensor
    t.xbuf = [sb(f"xbuf{i}", [128, NTT * D], F32) for i in range(2)]
    t.xlnT = sb("xlnT", [128, NDT * T], BF16)
    t.wugS = [sb(f"wugS{i}", [128, NDT * 1024], BF16) for i in range(2)]
    t.wdnS = [sb(f"wdnS{i}", [128, NET * 1024], BF16) for i in range(2)]
    t.hidden = sb("hidden", [128, NET * T], BF16)
    t.sgb = sb("sgb", [128, 1024], F32)
    t.ostY = sb("ostY", [128, 2048], F32)
    t.tsb = sb("tsb", [128, 1024], F32)
    t.dump = sb("dump", [128, 1024], F32)
    t.xln_tile = [sb(f"xlnt{i}", [128, 1024], BF16) for i in range(2)]
    t.cb = sb("cb", [128, CB_W], F32)
    t.identb = sb("identb_s", [128, 128], BF16)
    t.identf = sb("identf_s", [128, 128], F32)
    t.toki_s = sb("toki_s", [128, NTT], I32)
    t.rw_s = sb("rw_s", [128, NDT * 16], BF16)
    t.lb_s = sb("lb_s", [128, NDT * 32], BF16)
    t.wct_s = sb("wct_s", [16, D], BF16)
    t.zlamT = sb("zlamT", [16, T], F32)
    t.zbxT = sb("zbxT", [16, T], F32)
    t.rT = sb("rT", [16, T], F32)
    t.hs_bf = sb("hs_bf", [16, T], BF16)
    t.hcarry = sb("hcarry", [16, 1], F32)
    t.mT = sb("mT", [16, T], BF16)
    t.onesT = sb("onesT", [16, T], BF16)
    t.ones16 = sb("ones16", [16, 1], F32)
    t.onesr = sb("onesr", [1, 128], F32)
    t.zrt = sb("zrt", [128, NTT * 16], F32)
    t.top8 = sb("top8", [128, NTT * 8], F32)
    t.rsc = sb("rsc", [128, NTT * 4], F32)
    t.mask1a = sb("mask1a", [128, NTT * 16], F32)
    t.mask2a = sb("mask2a", [128, NTT * 16], F32)
    t.msk_t = sb("msk_t", [128, 16], F32)
    t.ch1 = sb("ch1", [128, 16], F32)
    t.stats = sb("stats", [128, NTT * 8], F32)
    t.sstat = sb("sstat", [128, NTT * 2], F32)
    t.zlb = sb("zlb", [128, NTT * 32], F32)
    t.totc = sb("totc", [16, 2], F32)
    t.posrow = sb("posrow", [1, 80], F32)
    t.padi = sb("padi", [1, 16], I32)
    t.padc = sb("padc", [1, 2], I32)
    t.pos16 = sb("pos16", [16, 1], F32)
    t.posb2 = sb("posb2", [128, 32], F32)
    t.pos2b = sb("pos2b", [128, 16], F32)
    t.cmpb = sb("cmpb", [16, 32], F32)
    t.rnkb = sb("rnkb", [128, 16], F32)
    t.basebuf = sb("basebuf", [128, 16], F32)
    t.slotf = sb("slotf", [128, 16], F32)
    t.sloti = sb("sloti", [128, 16], I32)
    t.idxwf = sb("idxwf", [128, NCH], F32)
    t.idxw_i = sb("idxw_i", [128, NCH], I32)
    t.xc = [sb(f"xc{i}", [128, 1024], BF16) for i in range(2)]
    t.xcT = [sb(f"xcT{i}", [128, 1024], BF16) for i in range(2)]
    t.hc = [sb(f"hc{i}", [128, 512], BF16) for i in range(2)]
    t.wsc = [sb(f"wsc{i}", [128, 1], F32) for i in range(2)]
    t.tokmap_i = [sb(f"tokmap{i}", [128, 1], I32) for i in range(2)]

    pt = nc.alloc_psum_tensor
    t.ps = [pt(f"ps{i}", [128, 1024], F32) for i in range(3)]
    t.psTR = pt("psTR", [128, 1024], BF16)
    t.psTR2 = pt("psTR2", [128, 1024], BF16)

    t.s_x = Sem(nc, "s_x", "dma_sp", 16)
    t.s_cst = Sem(nc, "s_cst", "dma_sp", 16)
    t.s_st = Sem(nc, "s_st", "dma_sp", 16)
    t.s_gw = Sem(nc, "s_gw", "dma_gp", 16)
    t.s_cv = [Sem(nc, f"s_cv{k}", "dma_gp", 16) for k in range(NEXP)]
    t.s_cvw = Sem(nc, "s_cvw", "dma_gp", 16)
    t.s_zz = Sem(nc, "s_zz", "dma_gp", 16)
    t.s_phu = [Sem(nc, f"s_phu{j}", "dma_sp", 16) for j in range(2)]
    t.s_phd = [Sem(nc, f"s_phd{j}", "dma_sp", 16) for j in range(2)]
    t.s_pgu = [Sem(nc, f"s_pgu{j}", "dma_gp", 16) for j in range(2)]
    t.s_pgd = [Sem(nc, f"s_pgd{j}", "dma_gp", 16) for j in range(2)]
    t.s_tw = [Sem(nc, f"s_tw{j}", "dma_sp", 16) for j in range(2)]
    t.s_xg = [Sem(nc, f"s_xg{j}", "dma_gp", 16) for j in range(2)]
    t.s_sc = Sem(nc, "s_sc", "dma_gp", 16)
    t.s_os = [Sem(nc, f"s_os{j}", "dma_sp", 16) for j in range(2)]
    t.s_xl = [Sem(nc, f"s_xl{j}", "dma_sp", 16) for j in range(2)]
    t.s_ga = [Sem(nc, f"s_ga{j}", "dma_gp", 16) for j in range(8)]
    t.s_pe = Sem(nc, "s_pe", "pe", 1)
    t.s_dve = Sem(nc, "s_dve", "dve", 1)
    t.s_act = Sem(nc, "s_act", "act", 1)
    return t


def conv_srcs(t, k):
    if k < NRE:
        return t.up_d[k], t.gate_d[k], t.down_d[k]
    j = k - NRE
    return (t.shu_t[:, j * 512:(j + 1) * 512],
            t.shg_t[:, j * 512:(j + 1) * 512],
            t.shd_t[j * 512:(j + 1) * 512, :])


def emit(P, t):
    s_x, s_st, s_cst, s_gw = t.s_x, t.s_st, t.s_cst, t.s_gw
    s_cv, s_cvw, s_zz = t.s_cv, t.s_cvw, t.s_zz
    s_phu, s_phd, s_pgu, s_pgd = t.s_phu, t.s_phd, t.s_pgu, t.s_pgd
    s_tw, s_xg, s_sc, s_os, s_xl, s_ga = (t.s_tw, t.s_xg, t.s_sc, t.s_os,
                                          t.s_xl, t.s_ga)
    s_pe, s_dve, s_act = t.s_pe, t.s_dve, t.s_act
    wexp2f = t.wexp2.rearrange("k p c -> (k p) c")
    wdn2f = t.wdn2.rearrange("k p c -> (k p) c")

    P.op("sp", [], lambda e: e.dma_start(out=t.cb[:], in_=t.cbias_d), (s_cst, "cb"))
    P.op("sp", [], lambda e: e.dma_start(out=t.identb[:], in_=t.identb_d), (s_cst, "idb"))
    P.op("sp", [], lambda e: e.dma_start(out=t.identf[:], in_=t.identf_d), (s_cst, "idf"))
    P.op("sp", [], lambda e: e.dma_start(out=t.toki_s[:], in_=t.toki_d), (s_cst, "tok"))
    CST = (s_cst, "tok")
    P.op("gp", [], lambda e: e.dma_start(
        out=t.rw_s.ap().rearrange("p (dt n) -> p dt n", n=16),
        in_=t.router_t.rearrange("(dt p) n -> p dt n", p=128)), (s_gw, "rw"))
    P.op("gp", [], lambda e: e.dma_start(
        out=t.lb_s.ap().rearrange("p (dt n) -> p dt n", n=32),
        in_=t.wlamb_t.rearrange("(dt p) n -> p dt n", p=128)), (s_gw, "lb"))
    P.op("gp", [], lambda e: e.dma_start(out=t.wct_s[:], in_=t.wct_d), (s_gw, "wct"))
    GW = (s_gw, "wct")
    for k in range(NEXP):
        us, gs, ds = conv_srcs(t, k)
        d3 = t.wexp2[k].rearrange("p (dt c) -> p dt c", c=1024)
        P.op("gp", [], lambda e, k=k, us=us, d3=d3: e.dma_start(
            out=d3[:, :, 0:512], in_=us.rearrange("(dt p) c -> p dt c", p=128)),
            (s_cv[k], f"cvu{k}"))
        P.op("gp", [], lambda e, k=k, gs=gs, d3=d3: e.dma_start(
            out=d3[:, :, 512:1024], in_=gs.rearrange("(dt p) c -> p dt c", p=128)),
            (s_cv[k], f"cvg{k}"))
        P.op("gp", [], lambda e, k=k, ds=ds: e.dma_start(
            out=t.wdn2[k].rearrange("p (et c) -> p et c", c=1024),
            in_=ds.rearrange("(et p) c -> p et c", p=128)),
            (s_cv[k], f"cvd{k}"))
    P.op("gp", [], lambda e: e.dma_start(
        out=t.wgate2.rearrange("p (dt c) -> p dt c", c=1024),
        in_=t.wgate_t.rearrange("(dt p) c -> p dt c", p=128)), (s_cvw, "cwg"))
    def init_dve(e):
        e.memset(t.hcarry[:], 0.0)
        e.memset(t.onesT[:], 1.0)
        e.memset(t.ones16[:], 1.0)
        e.memset(t.onesr[:], 1.0)
        e.memset(t.idxwf[:], 0.0)
        e.memset(t.padc[0:1, 0:1], 127)
        e.memset(t.padc[0:1, 1:2], -128)
        return e.memset(t.idxw_i[:], 0)
    P.op("dve", [], init_dve, (s_dve, "init"))
    P.op("gp", [(s_dve, "init")], lambda e: e.dma_start(
        out=t.tokmap_d.rearrange("(p c) w -> p (c w)", p=128), in_=t.idxw_i[:]),
        (s_zz, "zza"))
    P.op("gp", [], lambda e: e.dma_start(
        out=t.ws_d.rearrange("(p c) w -> p (c w)", p=128), in_=t.idxwf[:]),
        (s_zz, "zz"))

    st = NS()
    st.cc = 0
    st.dump_prev = [None]
    st.pair_free = {}
    st.pl = 0
    st.pool_done = {}
    st.pool_evu = {}
    st.pool_evd = {}
    st.sgfree = {}
    st.hcfree = {}
    st.xcev = {}
    st.xcTfree = {}
    st.osev = {}
    st.twev = {}
    st.xgev = {}
    st.galast = {}
    st.os_store = {}
    st.ga_prev = []

    def pair_waits(cc):
        return [st.pair_free[cc - 3]] if cc - 3 in st.pair_free else []

    def xload(b):
        xp = b % 2
        waits = [(s_st, f"y{b-2}")] if b >= 2 else []
        P.op("sp", waits, lambda e, b=b, xp=xp: e.dma_start(
            out=t.xbuf[xp].ap().rearrange("p (tt d) -> p tt d", d=D),
            in_=t.x_d[b * T:(b + 1) * T, :].rearrange("(tt p) d -> p tt d", p=128)),
            (s_x, f"x{b}"))

    def pool_load_regular(k, wdn=True):
        pl = st.pl
        st.pl += 1
        j = pl % 2
        w = [(s_cv[k] if k < NEXP else s_cvw, f"cvd{k}" if k < NEXP else "cwg")]
        if pl - 2 in st.pool_done:
            w.append(st.pool_done[pl - 2])
        if k < NEXP:
            src = t.wexp2[k]
        else:
            src = t.wgate2
        P.op("sp", w, lambda e, j=j, src=src: e.dma_start(
            out=t.wugS[j][:], in_=src), (s_phu[j], f"phu{pl}"))
        st.pool_evu[pl] = (s_phu[j], f"phu{pl}")
        if wdn:
            P.op("sp", [], lambda e, j=j, k=k: e.dma_start(
                out=t.wdnS[j][:], in_=t.wdn2[k]), (s_phd[j], f"phd{pl}"))
            st.pool_evd[pl] = (s_phd[j], f"phd{pl}")
        return pl, j

    def pool_load_chunk(c, idx_ev):
        pl = st.pl
        st.pl += 1
        j = pl % 2
        w = [idx_ev]
        if pl - 2 in st.pool_done:
            w.append(st.pool_done[pl - 2])
        w += [(s_cv[k], f"cvd{k}") for k in range(NRE)]
        P.op("gp", w, lambda e, j=j, c=c: e.indirect_dma_start(
            t.wugS[j][:], None, wexp2f,
            IndirectOffsetOnAxis(t.idxw_i[:, c:c + 1], 0)),
            (s_pgu[j], f"pgu{pl}"))
        st.pool_evu[pl] = (s_pgu[j], f"pgu{pl}")
        P.op("gp", [], lambda e, j=j, c=c: e.indirect_dma_start(
            t.wdnS[j][:], None, wdn2f,
            IndirectOffsetOnAxis(t.idxw_i[:, c:c + 1], 0)),
            (s_pgd[j], f"pgd{pl}"))
        st.pool_evd[pl] = (s_pgd[j], f"pgd{pl}")
        return pl, j

    def _xw_of(b, ph, tt):
        if ph == 0:
            if b == 0:
                return [(s_x, "x0")]
            return [(s_x, f"x{b}")]
        if ph == 1:
            return [(s_ga[tt], st.galast[(b, 0, tt)])]
        return [(s_dve, f"sadd{b}tt{tt}")]

    def _sc(tt):
        return t.stats[:, tt * 8:tt * 8 + 8]

    def ln_s1(b, ph, xp, tt):
        pre = f"{b}p{ph}"
        xw = _xw_of(b, ph, tt)
        x_ap = t.xbuf[xp][:, tt * D:(tt + 1) * D]
        P.op("dve", xw, lambda e: e.reduce_sum(
            _sc(tt)[:, 0:1], x_ap, axis=AX.X), (s_dve, f"lns{pre}t{tt}"))
        dmp = [st.dump_prev[0]] if st.dump_prev[0] is not None else []
        P.op("act", xw, lambda e: e.activation(
            t.dump[:], x_ap, AF.Square, accum_out=_sc(tt)[:, 1:2]),
            (s_act, f"lnq{pre}t{tt}"), fwaits=dmp)
        st.dump_prev[0] = (s_act, f"lnq{pre}t{tt}")
        P.op("act", [(s_dve, f"lns{pre}t{tt}"), CST],
             lambda e: e.activation(
                 _sc(tt)[:, 6:7], _sc(tt)[:, 0:1], AF.Square,
                 scale=t.cb[:, CB_INVD:CB_INVD + 1]),
             (s_act, f"lnq2{pre}t{tt}"))

    def ln_s24(b, ph):
        pre = f"{b}p{ph}"
        for tt in range(NTT):
            P.op("dve", [(s_act, f"lnq{pre}t{tt}"), (s_act, f"lnq2{pre}t{tt}")],
                 lambda e, tt=tt: e.scalar_tensor_tensor(
                     _sc(tt)[:, 3:4], _sc(tt)[:, 1:2], 1.0 / D,
                     _sc(tt)[:, 6:7], ALU.mult, ALU.subtract),
                 (s_dve, f"lnv{pre}t{tt}"))
        for tt in range(NTT):
            P.op("act", [(s_dve, f"lnv{pre}t{tt}"), CST],
                 lambda e, tt=tt: e.activation(
                     _sc(tt)[:, 4:5], _sc(tt)[:, 3:4], AF.Ln,
                     bias=t.cb[:, CB_EPS_LN:CB_EPS_LN + 1]),
                 (s_act, f"lnr1{pre}t{tt}"))
        for tt in range(NTT):
            P.op("act", [], lambda e, tt=tt: e.activation(
                _sc(tt)[:, 4:5], _sc(tt)[:, 4:5], AF.Exp,
                scale=t.cb[:, CB_MHALF:CB_MHALF + 1]),
                (s_act, f"lnr{pre}t{tt}"), fwaits=[(s_act, f"lnr1{pre}t{tt}")])
        for tt in range(NTT):
            P.op("dve", [(s_act, f"lnr{pre}t{tt}")],
                 lambda e, tt=tt: e.scalar_tensor_tensor(
                     _sc(tt)[:, 5:6], _sc(tt)[:, 0:1], -1.0 / D,
                     _sc(tt)[:, 4:5], ALU.mult, ALU.mult),
                 (s_dve, f"lnm{pre}t{tt}"))

    def ln_rest(b, ph, xp):
        pre = f"{b}p{ph}"

        def x_ap(tt):
            return t.xbuf[xp][:, tt * D:(tt + 1) * D]

        def norm_op(tt):
            nw = [(s_dve, f"lnm{pre}t{tt}")]
            if tt >= 2:
                nw.append((s_pe, f"tr{pre}t{tt-2}"))
                if ph != 1:
                    nw.append((s_xl[tt % 2], f"xl{pre}t{tt-2}"))
            P.op("act", nw, lambda e, tt=tt: e.activation(
                t.xln_tile[tt % 2][:], x_ap(tt), AF.Identity,
                bias=_sc(tt)[:, 5:6], scale=_sc(tt)[:, 4:5]),
                (s_act, f"lnn{pre}t{tt}"), fwaits=[(s_act, f"lnr{pre}t{tt}")])
            if ph != 1:
                xw2 = [(s_act, f"lnn{pre}t{tt}")]
                if tt == 0:
                    xw2 += [st.xgev[j] for j in st.xgev]
                P.op("sp", xw2, lambda e, tt=tt: e.dma_start(
                    out=t.xlnb_d[tt * 128:(tt + 1) * 128, :],
                    in_=t.xln_tile[tt % 2][:]),
                     (s_xl[tt % 2], f"xl{pre}t{tt}"))

        def trcopy_op(tt):
            ptr = t.psTR if tt % 2 == 0 else t.psTR2
            P.op("act", [(s_pe, f"tr{pre}t{tt}")],
                 lambda e, tt=tt, ptr=ptr: e.copy(
                     t.xlnT.ap().rearrange("p (dt c) -> p dt c", c=T)
                     [:, :, tt * 128:(tt + 1) * 128],
                     ptr.ap().rearrange("p (dt c) -> p dt c", c=128)),
                 (s_act, f"trc{pre}t{tt}"))

        for tt in range(NTT):
            norm_op(tt)
            if tt >= 1:
                trcopy_op(tt - 1)
        trcopy_op(NTT - 1)
        for tt in range(NTT):
            ptr = t.psTR if tt % 2 == 0 else t.psTR2
            trw = [(s_act, f"lnn{pre}t{tt}"), CST]
            if tt >= 2:
                trw.append((s_act, f"trc{pre}t{tt-2}"))

            def trops(e, tt=tt, ptr=ptr):
                last = None
                for dt in range(NDT):
                    last = e.transpose(ptr[:, dt * 128:(dt + 1) * 128],
                                       t.xln_tile[tt % 2][:, dt * 128:(dt + 1) * 128],
                                       t.identb[:])
                return last
            P.op("pe", trw, trops, (s_pe, f"tr{pre}t{tt}"))

        if ph != 1:
            router(b, ph)
            idx_build(b, ph)

    def router(b, ph):
        pre = f"{b}p{ph}"
        cc = st.cc
        st.cc += 1
        ps = t.ps[cc % 3]
        for tt in range(NTT):
            w = [(s_act, f"trc{pre}t{tt}"), GW]
            if tt == 0:
                w += pair_waits(cc)

            def rops(e, tt=tt, ps=ps):
                last = None
                for dt in range(NDT):
                    last = e.matmul(ps[:, tt * 16:(tt + 1) * 16],
                                    t.xlnT[:, dt * T + tt * 128: dt * T + (tt + 1) * 128],
                                    t.rw_s[:, dt * 16:(dt + 1) * 16],
                                    start=(dt == 0), stop=(dt == NDT - 1))
                return last
            P.op("pe", w, rops,
                 (s_pe, f"pf{cc}") if tt == NTT - 1 else None)
        P.op("dve", [(s_pe, f"pf{cc}"), CST],
             lambda e, ps=ps: e.tensor_add(t.zrt[:], ps[:, 0:NTT * 16],
                                           t.cb[:, CB_RB8:CB_RB8 + NTT * 16]),
             (s_dve, f"pd{cc}"))
        st.pair_free[cc] = (s_dve, f"pd{cc}")

        def sl(buf, tt, wdt):
            return buf[:, tt * wdt:(tt + 1) * wdt]

        for tt in range(NTT):
            P.op("dve", [], lambda e, tt=tt: e.max(sl(t.top8, tt, 8),
                                                   sl(t.zrt, tt, 16)),
                 (s_dve, f"rmx{pre}t{tt}"), fwaits=[(s_dve, f"pd{cc}")])
        for tt in range(NTT):
            P.op("dve", [], lambda e, tt=tt: e.tensor_sub(
                sl(t.rsc, tt, 4)[:, 0:1], sl(t.top8, tt, 8)[:, 0:1],
                sl(t.top8, tt, 8)[:, 1:2]),
                (s_dve, f"d12{pre}t{tt}"), fwaits=[(s_dve, f"rmx{pre}t{tt}")])
        for tt in range(NTT):
            P.op("act", [(s_dve, f"d12{pre}t{tt}")],
                 lambda e, tt=tt: e.activation(sl(t.rsc, tt, 4)[:, 1:2],
                                               sl(t.rsc, tt, 4)[:, 0:1],
                                               AF.Sigmoid),
                 (s_act, f"s1{pre}t{tt}"))
        for tt in range(NTT):
            def mraw(e, tt=tt):
                rs = sl(t.rsc, tt, 4)
                e.tensor_scalar_mul(rs[:, 2:3], rs[:, 1:2], 0.5)
                e.tensor_scalar(rs[:, 3:4], rs[:, 1:2], -0.5, 0.5,
                                ALU.mult, ALU.add)
                e.tensor_scalar(sl(t.mask1a, tt, 16), sl(t.zrt, tt, 16),
                                sl(t.top8, tt, 8)[:, 0:1], None, ALU.is_equal)
                return e.tensor_scalar(sl(t.mask2a, tt, 16), sl(t.zrt, tt, 16),
                                       sl(t.top8, tt, 8)[:, 1:2], None,
                                       ALU.is_equal)
            P.op("dve", [(s_act, f"s1{pre}t{tt}")], mraw,
                 (s_dve, f"mr{pre}t{tt}") if tt == NTT - 1 else None)
        prevs = (s_dve, f"mr{pre}t{NTT-1}")
        for tt in range(NTT):
            P.op("dve", [], lambda e, tt=tt: e.tensor_tensor_scan(
                t.ch1[:], t.cb[:, CB_ONES16:CB_ONES16 + 16],
                sl(t.mask1a, tt, 16), 0.0, ALU.mult, ALU.add),
                (s_dve, f"saS{pre}t{tt}"), fwaits=[prevs])
            P.op("dve", [], lambda e: e.tensor_scalar(
                t.msk_t[:], t.ch1[:], 1.0, None, ALU.is_equal),
                (s_dve, f"saE{pre}t{tt}"), fwaits=[(s_dve, f"saS{pre}t{tt}")])
            P.op("dve", [], lambda e, tt=tt: e.tensor_mul(
                sl(t.mask1a, tt, 16), sl(t.mask1a, tt, 16), t.msk_t[:]),
                (s_dve, f"sa{pre}t{tt}"), fwaits=[(s_dve, f"saE{pre}t{tt}")])
            prevs = (s_dve, f"sa{pre}t{tt}")
        prevs = (s_dve, f"sa{pre}t{NTT-1}")
        for tt in range(NTT):
            P.op("dve", [], lambda e, tt=tt: e.tensor_mul(
                t.msk_t[:], sl(t.mask1a, tt, 16), sl(t.mask2a, tt, 16)),
                (s_dve, f"sbA{pre}t{tt}"), fwaits=[prevs])
            P.op("dve", [], lambda e, tt=tt: e.tensor_sub(
                sl(t.mask2a, tt, 16), sl(t.mask2a, tt, 16), t.msk_t[:]),
                (s_dve, f"sbB{pre}t{tt}"), fwaits=[(s_dve, f"sbA{pre}t{tt}")])
            P.op("dve", [], lambda e, tt=tt: e.tensor_tensor_scan(
                t.ch1[:], t.cb[:, CB_ONES16:CB_ONES16 + 16],
                sl(t.mask2a, tt, 16), 0.0, ALU.mult, ALU.add),
                (s_dve, f"sbC{pre}t{tt}"), fwaits=[(s_dve, f"sbB{pre}t{tt}")])
            P.op("dve", [], lambda e: e.tensor_scalar(
                t.msk_t[:], t.ch1[:], 1.0, None, ALU.is_equal),
                (s_dve, f"sbD{pre}t{tt}"), fwaits=[(s_dve, f"sbC{pre}t{tt}")])
            P.op("dve", [], lambda e, tt=tt: e.tensor_mul(
                sl(t.mask2a, tt, 16), sl(t.mask2a, tt, 16), t.msk_t[:]),
                (s_dve, f"sb{pre}t{tt}"), fwaits=[(s_dve, f"sbD{pre}t{tt}")])
            prevs = (s_dve, f"sb{pre}t{tt}")

    def idx_build(b, ph):
        pre = f"{b}p{ph}"
        p = 0 if ph == 0 else 1

        def mtr_chunk(masks, tag, sanev, extra=()):
            cc = st.cc
            st.cc += 1
            ps = t.ps[cc % 3]

            def trs(e, masks=masks, ps=ps):
                last = None
                for tt in range(NTT):
                    last = e.transpose(ps[0:16, tt * 128:(tt + 1) * 128],
                                       masks[:, tt * 16:(tt + 1) * 16],
                                       t.identf[:])
                return last
            P.op("pe", [sanev, CST] + list(extra) + pair_waits(cc), trs,
                 (s_pe, f"pf{cc}"))
            P.op("act", [(s_pe, f"pf{cc}")],
                 lambda e, ps=ps: e.copy(t.mT[:], ps[0:16, 0:T]),
                 (s_act, f"pd{cc}"))
            st.pair_free[cc] = (s_act, f"pd{cc}")
            return (s_act, f"pd{cc}")

        def one_list(li, masks, sanev):
            if li == 1:
                mtev = mtr_chunk(masks, "m1",
                                 sanev, extra=[(s_dve, f"pr0{pre}")])
            else:
                mtev = mtr_chunk(masks, "m0", sanev)
            P.op("dve", [mtev], lambda e: e.tensor_tensor_scan(
                t.rT[:], t.onesT[:], t.mT[:], 0.0, ALU.mult, ALU.add),
                (s_dve, f"rs{li}{pre}"))
            P.op("dve", [], lambda e, li=li: e.tensor_copy(
                t.totc[:, li:li + 1], t.rT[:, T - 1:T]),
                (s_dve, f"tc{li}{pre}"), fwaits=[(s_dve, f"rs{li}{pre}")])
            P.op("dve", [], lambda e: e.tensor_mul(t.rT[:], t.rT[:], t.mT.ap()),
                 (s_dve, f"pr{li}{pre}"), fwaits=[(s_dve, f"tc{li}{pre}")])
            cc = st.cc
            st.cc += 1
            ps = t.ps[cc % 3]

            def rnk(e, li=li, ps=ps):
                last = None
                for tt in range(NTT):
                    last = e.matmul(ps[:, li * 8 + tt:li * 8 + tt + 1],
                                    t.rT[:, tt * 128:(tt + 1) * 128],
                                    t.ones16[:], start=True, stop=True)
                return last
            P.op("pe", [(s_dve, f"pr{li}{pre}")] + pair_waits(cc), rnk,
                 (s_pe, f"pf{cc}"))
            P.op("dve", [(s_pe, f"pf{cc}")],
                 lambda e, li=li, ps=ps: e.tensor_scalar(
                     t.rnkb[:, li * 8:(li + 1) * 8], ps[:, li * 8:(li + 1) * 8],
                     1.0, -1.0, ALU.mult, ALU.add),
                 (s_dve, f"pd{cc}"))
            st.pair_free[cc] = (s_dve, f"pd{cc}")
            return (s_dve, f"pd{cc}")

        rk1 = one_list(0, t.mask1a, (s_dve, f"sa{pre}t{NTT-1}"))
        rk2 = one_list(1, t.mask2a, (s_dve, f"sb{pre}t{NTT-1}"))
        P.op("dve", [], lambda e: e.tensor_add(
            t.totc[:, 1:2], t.totc[:, 0:1], t.totc[:, 1:2]),
            (s_dve, f"tt{pre}"), fwaits=[(s_dve, f"tc1{pre}")])
        cc = st.cc
        st.cc += 1
        ps = t.ps[cc % 3]

        def tottr(e, ps=ps):
            e.transpose(ps[0:1, 16:32], t.totc[:, 0:1], t.identf[0:16, 0:16])
            return e.transpose(ps[0:1, 32:48], t.totc[:, 1:2],
                               t.identf[0:16, 0:16])
        P.op("pe", [(s_dve, f"tt{pre}")] + pair_waits(cc), tottr, (s_pe, f"pf{cc}"))
        P.op("act", [(s_pe, f"pf{cc}")],
             lambda e, ps=ps: e.copy(t.posrow[0:1, 16:48], ps[0:1, 16:48]),
             (s_act, f"pd{cc}"))
        st.pair_free[cc] = (s_act, f"pd{cc}")
        totr_ev = (s_act, f"pd{cc}")
        pr = t.posrow

        P.op("dve", [totr_ev], lambda e: e.tensor_scalar(
            pr[0:1, 48:64], pr[0:1, 32:48], 127.0, None, ALU.add),
            (s_dve, f"pm0{pre}"))
        P.op("dve", [], lambda e: e.tensor_copy(
            t.padi[:], pr[0:1, 48:64]), (s_dve, f"pm1{pre}"),
            fwaits=[(s_dve, f"pm0{pre}")])
        P.op("dve", [], lambda e: e.tensor_scalar(
            t.padi[:], t.padi[:], t.padc[0:1, 1:2], None, ALU.bitwise_and),
            (s_dve, f"pm2{pre}"), fwaits=[(s_dve, f"pm1{pre}")])
        P.op("dve", [], lambda e: e.tensor_copy(
            pr[0:1, 48:64], t.padi[:]),
            (s_dve, f"pm4{pre}"), fwaits=[(s_dve, f"pm2{pre}")])
        P.op("dve", [], lambda e: e.tensor_tensor_scan(
            pr[0:1, 64:80], t.onesr[:, 0:16], pr[0:1, 48:64], 0.0,
            ALU.mult, ALU.add),
            (s_dve, f"pm5{pre}"), fwaits=[(s_dve, f"pm4{pre}")])
        P.op("dve", [], lambda e: e.tensor_sub(
            pr[0:1, 0:16], pr[0:1, 64:80], pr[0:1, 48:64]),
            (s_dve, f"pos{pre}"), fwaits=[(s_dve, f"pm5{pre}")])
        cc = st.cc
        st.cc += 1
        ps = t.ps[cc % 3]
        P.op("pe", [(s_dve, f"pos{pre}")] + pair_waits(cc),
             lambda e, ps=ps: e.transpose(ps[0:16, 64:65], pr[0:1, 0:16],
                                          t.identf[0:1, 0:1]),
             (s_pe, f"pf{cc}"))
        P.op("act", [(s_pe, f"pf{cc}")],
             lambda e, ps=ps: e.copy(t.pos16[:], ps[0:16, 64:65]),
             (s_act, f"pd{cc}"))
        st.pair_free[cc] = (s_act, f"pd{cc}")
        pos16_ev = (s_act, f"pd{cc}")
        cc = st.cc
        st.cc += 1
        ps = t.ps[cc % 3]
        P.op("pe", [(s_dve, f"pos{pre}")] + pair_waits(cc),
             lambda e, ps=ps: e.matmul(ps[:, 0:32], t.onesr[:],
                                       pr[0:1, 0:32], start=True, stop=True),
             (s_pe, f"pf{cc}"))
        P.op("dve", [(s_pe, f"pf{cc}")],
             lambda e, ps=ps: e.tensor_copy(t.posb2[:], ps[:, 0:32]),
             (s_dve, f"pd{cc}"))
        st.pair_free[cc] = (s_dve, f"pd{cc}")
        P.op("dve", [], lambda e: e.tensor_add(
            t.pos2b[:], t.posb2[:, 0:16], t.posb2[:, 16:32]),
            (s_dve, f"p2b{pre}"), fwaits=[(s_dve, f"pd{cc}")])
        for tt in range(NTT):
            for li in range(2):
                masks = t.mask1a if li == 0 else t.mask2a
                posb = t.posb2[:, 0:16] if li == 0 else t.pos2b[:]
                g = tt * 2 + li
                P.op("dve", [], lambda e, tt=tt, masks=masks, posb=posb:
                     e.tensor_mul(t.msk_t[:], masks[:, tt * 16:(tt + 1) * 16],
                                  posb),
                     (s_dve, f"bm{g}{pre}"),
                     fwaits=([(s_dve, f"p2b{pre}")] if g == 0 else
                             [(s_dve, f"bs{g-1}{pre}")]))
                P.op("dve", [], lambda e, g=g: e.reduce_sum(
                    t.basebuf[:, g:g + 1], t.msk_t[:], axis=AX.X),
                    (s_dve, f"bs{g}{pre}"),
                    fwaits=[(s_dve, f"bm{g}{pre}")])
        for tt in range(NTT):
            for li in range(2):
                g = tt * 2 + li
                P.op("dve", [], lambda e, tt=tt, li=li, g=g:
                     e.scalar_tensor_tensor(
                         t.slotf[:, g:g + 1], t.rnkb[:, li * 8 + tt:li * 8 + tt + 1],
                         1.0, t.basebuf[:, g:g + 1], ALU.mult, ALU.add),
                     (s_dve, f"slotf{pre}") if g == NTT * 2 - 1 else None,
                     fwaits=[(s_dve, f"bs{NTT*2-1}{pre}")] if g == 0 else ())
        P.op("dve", list(st.ga_prev),
             lambda e: e.tensor_copy(t.sloti[:], t.slotf[:]),
             (s_dve, f"sloti{pre}"), fwaits=[(s_dve, f"slotf{pre}")])
        P.op("dve", [pos16_ev, CST], lambda e: e.tensor_scalar(
            t.cmpb[:], t.cb[0:16, CB_C128:CB_C128 + NCH], t.pos16[:, 0:1],
            None, ALU.is_ge), (s_dve, f"cmp{pre}"))
        cc = st.cc
        st.cc += 1
        ps = t.ps[cc % 3]
        P.op("pe", [(s_dve, f"cmp{pre}")] + pair_waits(cc),
             lambda e, ps=ps: e.matmul(ps[0:1, 32:64], t.ones16[:],
                                       t.cmpb[:], start=True, stop=True),
             (s_pe, f"pf{cc}"))
        P.op("dve", [(s_pe, f"pf{cc}")], lambda e, ps=ps: e.tensor_scalar(
            pr[0:1, 48:80], ps[0:1, 32:64], 128.0, -128.0, ALU.mult, ALU.add),
            (s_dve, f"pd{cc}"))
        st.pair_free[cc] = (s_dve, f"pd{cc}")
        cc = st.cc
        st.cc += 1
        ps = t.ps[cc % 3]
        P.op("pe", [(s_dve, f"pd{cc-1}")] + pair_waits(cc),
             lambda e, ps=ps: e.matmul(ps[:, 0:32], t.onesr[:],
                                       pr[0:1, 48:80], start=True, stop=True),
             (s_pe, f"pf{cc}"))
        P.op("dve", [(s_pe, f"pf{cc}"), CST, (s_zz, "zz")],
             lambda e, ps=ps: e.tensor_scalar(
                 t.idxwf[:], ps[:, 0:32], t.cb[:, CB_IOTA:CB_IOTA + 1], None,
                 ALU.add), (s_dve, f"pd{cc}"))
        st.pair_free[cc] = (s_dve, f"pd{cc}")
        P.op("dve", list(getattr(st, "wg_final", [])),
             lambda e: e.tensor_copy(t.idxw_i[:], t.idxwf[:]),
             (s_dve, f"idxw{b}p{p}"), fwaits=[(s_dve, f"pd{cc}")])
        nsc = 0
        for tt in range(NTT):
            for li in range(2):
                g = tt * 2 + li
                wsl = (t.rsc[:, tt * 4 + 2:tt * 4 + 3] if li == 0
                       else t.rsc[:, tt * 4 + 3:tt * 4 + 4])
                w = [(s_dve, f"sloti{pre}"), CST, (s_zz, "zz")]
                if g == 0:
                    w += list(getattr(st, "tw_final", []))
                P.op("gp", w, lambda e, tt=tt, g=g: e.indirect_dma_start(
                    t.tokmap_d, IndirectOffsetOnAxis(t.sloti[:, g:g + 1], 0),
                    t.toki_s[:, tt:tt + 1], None),
                    (s_sc, f"sc{pre}g{g}a"))
                P.op("gp", [], lambda e, g=g, wsl=wsl: e.indirect_dma_start(
                    t.ws_d, IndirectOffsetOnAxis(t.sloti[:, g:g + 1], 0),
                    wsl, None),
                    (s_sc, f"sc{pre}g{g}b"))
                nsc += 2
        st.scat_ev = (s_sc, f"sc{pre}g{NTT*2-1}b")

    def shared_fills(k, pl, j, lnph, b):
        lastug = None
        for et in range(NET):
            for tc in range(T // 512):
                cc = st.cc
                st.cc += 1
                ps = t.ps[cc % 3]
                w = [st.pool_evu[pl],
                     (s_act, f"trc{b}p{lnph}t{tc * 4 + 3}")]
                w += pair_waits(cc)
                if k == NEXP - 1:
                    w.append(st.pool_done.get(pl - 1, st.pool_evu[pl]))
                sgp = cc % 2

                def fillug(e, j=j, et=et, tc=tc, ps=ps):
                    last = None
                    for half in (0, 1):
                        for dt in range(NDT):
                            wcol = dt * 1024 + half * 512 + et * 128
                            last = e.matmul(
                                ps[:, half * 512:half * 512 + 512],
                                t.wugS[j][:, wcol:wcol + 128],
                                t.xlnT[:, dt * T + tc * 512: dt * T + (tc + 1) * 512],
                                start=(dt == 0), stop=(dt == NDT - 1))
                    return last
                P.op("pe", w, fillug, (s_pe, f"pf{cc}"))
                sw = [(s_pe, f"pf{cc}")]
                if sgp in st.sgfree:
                    sw.append(st.sgfree[sgp])
                P.op("act", sw, lambda e, ps=ps, sgp=sgp: e.activation(
                    t.sgb[:, sgp * 512:(sgp + 1) * 512], ps[:, 512:1024],
                    AF.Silu), (s_act, f"sil{cc}"))
                hcol = et * T + tc * 512
                P.op("dve", [(s_pe, f"pf{cc}"), (s_act, f"sil{cc}")],
                     lambda e, ps=ps, sgp=sgp, hcol=hcol: e.tensor_mul(
                         t.hidden[:, hcol:hcol + 512], ps[:, 0:512],
                         t.sgb[:, sgp * 512:(sgp + 1) * 512]),
                     (s_dve, f"pd{cc}"))
                st.pair_free[cc] = (s_dve, f"pd{cc}")
                st.sgfree[sgp] = (s_dve, f"pd{cc}")
                lastug = cc
        return (s_dve, f"pd{lastug}")

    def shared_downs(k, pl, j, hid_done, b, p, xp):
        for tt in range(NTT):
            cc = st.cc
            st.cc += 1
            ps = t.ps[cc % 3]
            w = [hid_done, st.pool_evd[pl]] + pair_waits(cc)

            def filldn(e, j=j, tt=tt, ps=ps):
                last = None
                for fc in range(2):
                    for et in range(NET):
                        last = e.matmul(
                            ps[:, fc * 512:(fc + 1) * 512],
                            t.hidden[:, et * T + tt * 128: et * T + (tt + 1) * 128],
                            t.wdnS[j][:, et * 1024 + fc * 512: et * 1024 + (fc + 1) * 512],
                            start=(et == 0), stop=(et == NET - 1))
                return last
            P.op("pe", w, filldn, (s_pe, f"pf{cc}"))
            if tt == NTT - 1:
                st.pool_done[pl] = (s_pe, f"pf{cc}")
            xt = t.xbuf[xp][:, tt * D:(tt + 1) * D]
            P.op("dve", [(s_pe, f"pf{cc}")],
                 lambda e, ps=ps, xt=xt: e.scalar_tensor_tensor(
                     xt, ps[:], 0.5, xt, ALU.mult, ALU.add),
                 (s_dve, f"pd{cc}"))
            st.pair_free[cc] = (s_dve, f"pd{cc}")
            P.alias(s_dve, f"shd{b}p{p}k{k}tt{tt}", f"pd{cc}")

    def sparse_chunks(b, p, xp, lnph):
        pre = f"{b}p{p}"
        idx_ev = (s_dve, f"idxw{b}p{p}")
        xl_done = [(s_xl[0], f"xl{b}p{lnph}t{NTT-2}"),
                   (s_xl[1], f"xl{b}p{lnph}t{NTT-1}")]
        ch_pls = []
        for c in range(NCH):
            pl, j = pool_load_chunk(c, idx_ev)
            ch_pls.append(pl)
            cp = c % 2
            tw = [st.scat_ev]
            if cp in st.xgev:
                tw.append(st.xgev[cp])
            if cp in st.osev:
                tw.append((s_dve, st.osev[cp]))

            P.op("sp", tw, lambda e, c=c, cp=cp: e.dma_start(
                out=t.tokmap_i[cp][:],
                in_=t.tokmap_d[c * 128:(c + 1) * 128, :]),
                (s_tw[cp], f"twa{pre}c{c}"))
            P.op("sp", [], lambda e, c=c, cp=cp: e.dma_start(
                out=t.wsc[cp][:],
                in_=t.ws_d[c * 128:(c + 1) * 128, :]),
                (s_tw[cp], f"tw{pre}c{c}"))
            st.twev[cp] = (s_tw[cp], f"tw{pre}c{c}")
            gw = [st.twev[cp]] + xl_done
            if cp in st.xcev:
                gw.append(st.xcev[cp])
            P.op("gp", gw, lambda e, c=c, cp=cp: e.indirect_dma_start(
                t.xc[cp][:], None, t.xlnb_d,
                IndirectOffsetOnAxis(t.tokmap_i[cp][:, 0:1], 0)),
                (s_xg[cp], f"xg{pre}c{c}"))
            st.xgev[cp] = (s_xg[cp], f"xg{pre}c{c}")
            ptr = t.psTR if cp == 0 else t.psTR2
            trw = [st.xgev[cp], (s_act, f"trc{b}p{lnph}t{NTT-1}"), CST]
            if cp in st.xcTfree:
                trw.append(st.xcTfree[cp])

            def ctr(e, cp=cp, ptr=ptr):
                last = None
                for dt in range(NDT):
                    last = e.transpose(ptr[:, dt * 128:(dt + 1) * 128],
                                       t.xc[cp][:, dt * 128:(dt + 1) * 128],
                                       t.identb[:])
                return last
            P.op("pe", trw, ctr, (s_pe, f"ctr{pre}c{c}"))
            st.xcev[cp] = (s_pe, f"ctr{pre}c{c}")
            P.op("act", [(s_pe, f"ctr{pre}c{c}")],
                 lambda e, cp=cp, ptr=ptr: e.copy(
                     t.xcT[cp].ap().rearrange("p (dt c) -> p dt c", c=128),
                     ptr.ap().rearrange("p (dt c) -> p dt c", c=128)),
                 (s_act, f"ctc{pre}c{c}"))
            st.xcTfree[cp] = (s_act, f"ctc{pre}c{c}")
            cc = st.cc
            st.cc += 1
            ps = t.ps[cc % 3]
            w = [st.pool_evu[pl], (s_act, f"ctc{pre}c{c}")] + pair_waits(cc)
            sgp = cc % 2

            def cfill(e, j=j, cp=cp, ps=ps):
                last = None
                for half in (0, 1):
                    for et in range(NET):
                        for dt in range(NDT):
                            wcol = dt * 1024 + half * 512 + et * 128
                            last = e.matmul(
                                ps[:, half * 512 + et * 128: half * 512 + (et + 1) * 128],
                                t.wugS[j][:, wcol:wcol + 128],
                                t.xcT[cp][:, dt * 128:(dt + 1) * 128],
                                start=(dt == 0), stop=(dt == NDT - 1))
                return last
            P.op("pe", w, cfill, (s_pe, f"pf{cc}"))
            sw = [(s_pe, f"pf{cc}")]
            if sgp in st.sgfree:
                sw.append(st.sgfree[sgp])
            P.op("act", sw, lambda e, ps=ps, sgp=sgp: e.activation(
                t.sgb[:, sgp * 512:(sgp + 1) * 512], ps[:, 512:1024], AF.Silu),
                (s_act, f"sil{cc}"))
            dw = [(s_pe, f"pf{cc}"), (s_act, f"sil{cc}")]
            if cp in st.hcfree:
                dw.append(st.hcfree[cp])
            P.op("dve", dw, lambda e, ps=ps, sgp=sgp, cp=cp: e.tensor_mul(
                t.hc[cp][:], ps[:, 0:512],
                t.sgb[:, sgp * 512:(sgp + 1) * 512]), (s_dve, f"pd{cc}"))
            st.pair_free[cc] = (s_dve, f"pd{cc}")
            st.sgfree[sgp] = (s_dve, f"pd{cc}")
            hc_ready = (s_dve, f"pd{cc}")
            cc2 = st.cc
            st.cc += 1
            ps2 = t.ps[cc2 % 3]
            w = [hc_ready, st.pool_evd[pl]] + pair_waits(cc2)

            def cdown(e, j=j, cp=cp, ps2=ps2):
                last = None
                for fc in range(2):
                    for et in range(NET):
                        last = e.matmul(
                            ps2[:, fc * 512:(fc + 1) * 512],
                            t.hc[cp][:, et * 128:(et + 1) * 128],
                            t.wdnS[j][:, et * 1024 + fc * 512: et * 1024 + (fc + 1) * 512],
                            start=(et == 0), stop=(et == NET - 1))
                return last
            P.op("pe", w, cdown, (s_pe, f"pf{cc2}"))
            st.pool_done[pl] = (s_pe, f"pf{cc2}")
            st.hcfree[cp] = (s_pe, f"pf{cc2}")
            ow = [(s_pe, f"pf{cc2}"), st.twev[cp]]
            if c == 0:
                ow += list(st.ga_prev)
            if st.os_store.get(cp) is not None:
                ow.append(st.os_store[cp])
            ost = t.ostY[:, cp * 1024:(cp + 1) * 1024]
            P.op("dve", ow, lambda e, ps2=ps2, ost=ost, cp=cp: e.tensor_scalar(
                ost, ps2[:], t.wsc[cp][:, 0:1], None, ALU.mult),
                (s_dve, f"pd{cc2}"))
            st.pair_free[cc2] = (s_dve, f"pd{cc2}")
            st.osev[cp] = f"pd{cc2}"
            P.op("sp", [(s_dve, f"pd{cc2}")],
                 lambda e, c=c, ost=ost: e.dma_start(
                     out=t.out_d[c * 128:(c + 1) * 128, :], in_=ost),
                 (s_os[cp], f"os{pre}c{c}"))
            st.os_store[cp] = (s_os[cp], f"os{pre}c{c}")
        st.os_final = [st.os_store[0], st.os_store[1]]
        st.tw_final = [st.twev[0], st.twev[1]]
        st.wg_final = [st.pool_evu[ch_pls[-1]], st.pool_evd[ch_pls[-1]],
                       st.pool_evu[ch_pls[-2]], st.pool_evd[ch_pls[-2]]]

    def gathers(b, p, xp, tail_s1=None):
        pre = f"{b}p{p}"
        lnph = 0 if p == 0 else 2
        for tt in range(NTT):
            w = list(st.os_final)
            w.append((s_dve, f"shd{b}p{p}k{NEXP-1}tt{tt}"))
            w.append((s_act, f"lnn{b}p{lnph}t{tt}"))
            w.append((s_dve, f"lns{b}p{lnph}t{tt}"))
            xt = t.xbuf[xp][:, tt * D:(tt + 1) * D]
            P.op("gp", w, lambda e, tt=tt, xt=xt: e.indirect_dma_start(
                xt, None, t.out_d,
                IndirectOffsetOnAxis(t.sloti[:, tt * 2:tt * 2 + 1], 0),
                compute_op=ALU.add),
                (s_ga[tt], f"ga1{pre}t{tt}"))
            P.op("gp", [(s_ga[tt], f"ga1{pre}t{tt}")],
                 lambda e, tt=tt, xt=xt: e.indirect_dma_start(
                     xt, None, t.out_d,
                     IndirectOffsetOnAxis(t.sloti[:, tt * 2 + 1:tt * 2 + 2], 0),
                     compute_op=ALU.add),
                 (s_ga[tt], f"ga2{pre}t{tt}"))
            st.galast[(b, p, tt)] = f"ga2{pre}t{tt}"
            if tail_s1 is not None:
                tail_s1(tt)
        st.ga_prev = [(s_ga[tt], f"ga2{pre}t{tt}") for tt in range(NTT)]

    def ffn_pass(b, p, xp, tail_s1=None, tail_s24=None):
        lnph = 0 if p == 0 else 2
        pl16, j16 = pool_load_regular(16)
        pl17, j17 = pool_load_regular(17)
        hd16 = shared_fills(16, pl16, j16, lnph, b)
        shared_downs(16, pl16, j16, hd16, b, p, xp)
        hd17 = shared_fills(NEXP - 1, pl17, j17, lnph, b)
        shared_downs(17, pl17, j17, hd17, b, p, xp)
        sparse_chunks(b, p, xp, lnph)
        gathers(b, p, xp, tail_s1=tail_s1)
        if tail_s24 is not None:
            tail_s24()

    def ssm_phase(b, xp):
        ln_rest(b, 1, xp)
        plw, jw = pool_load_regular(NEXP, wdn=False)
        ccZ = st.cc
        st.cc += 1
        psZ = t.ps[ccZ % 3]
        for tt in range(NTT):
            w = [(s_act, f"trc{b}p1t{tt}"), GW]
            if tt == 0:
                w += pair_waits(ccZ)

            def lbops(e, tt=tt, psZ=psZ):
                last = None
                for dt in range(NDT):
                    last = e.matmul(psZ[:, tt * 32:(tt + 1) * 32],
                                    t.xlnT[:, dt * T + tt * 128: dt * T + (tt + 1) * 128],
                                    t.lb_s[:, dt * 32:(dt + 1) * 32],
                                    start=(dt == 0), stop=(dt == NDT - 1))
                return last
            P.op("pe", w, lbops, (s_pe, f"pf{ccZ}") if tt == NTT - 1 else None)
        P.op("dve", [(s_pe, f"pf{ccZ}"), CST],
             lambda e, psZ=psZ: e.tensor_add(
                 t.zlb[:], psZ[:, 0:NTT * 32],
                 t.cb[:, CB_BL8:CB_BL8 + NTT * 32]),
             (s_dve, f"pd{ccZ}"))
        st.pair_free[ccZ] = (s_dve, f"pd{ccZ}")
        ccL = st.cc
        st.cc += 1
        psL = t.ps[ccL % 3]

        def latr(e, psL=psL):
            last = None
            for tt in range(NTT):
                last = e.transpose(psL[0:16, tt * 128:(tt + 1) * 128],
                                   t.zlb[:, tt * 32:tt * 32 + 16], t.identf[:])
            return last
        P.op("pe", [(s_dve, f"pd{ccZ}"), CST] + pair_waits(ccL),
             latr, (s_pe, f"pf{ccL}"))
        P.op("act", [(s_pe, f"pf{ccL}")],
             lambda e, psL=psL: e.copy(t.zlamT[:], psL[0:16, 0:T]),
             (s_act, f"pd{ccL}"))
        st.pair_free[ccL] = (s_act, f"pd{ccL}")
        ccB = st.cc
        st.cc += 1
        psB = t.ps[ccB % 3]

        def bxtr(e, psB=psB):
            last = None
            for tt in range(NTT):
                last = e.transpose(psB[0:16, tt * 128:(tt + 1) * 128],
                                   t.zlb[:, tt * 32 + 16:tt * 32 + 32], t.identf[:])
            return last
        P.op("pe", pair_waits(ccB), bxtr, (s_pe, f"pf{ccB}"))
        P.op("act", [(s_pe, f"pf{ccB}")],
             lambda e, psB=psB: e.copy(t.zbxT[:], psB[0:16, 0:T]),
             (s_act, f"pd{ccB}"))
        st.pair_free[ccB] = (s_act, f"pd{ccB}")
        P.op("act", [], lambda e: e.activation(t.zlamT[:], t.zlamT[:], AF.Sigmoid),
             (s_act, f"sig{b}"), fwaits=[(s_act, f"pd{ccL}")])
        P.op("dve", [(s_act, f"sig{b}"), (s_act, f"pd{ccB}")],
             lambda e: e.tensor_tensor_scan(t.rT[:], t.zlamT[:], t.zbxT[:],
                                            t.hcarry[:, 0:1], ALU.mult, ALU.add),
             (s_dve, f"scan{b}"))

        def hscopy(e):
            e.tensor_copy(t.hcarry[:, 0:1], t.rT[:, T - 1:T])
            return e.tensor_copy(t.hs_bf[:], t.rT[:])
        P.op("dve", [], hscopy, (s_dve, f"hsbf{b}"), fwaits=[(s_dve, f"scan{b}")])

        for g in range(NTT // 2):
            tts = range(g * 2, g * 2 + 2)
            for tt in tts:
                ccg = st.cc
                st.cc += 1
                psg = t.ps[ccg % 3]
                w = [st.pool_evu[plw], (s_act, f"trc{b}p1t{tt}")] + pair_waits(ccg)

                def gops(e, tt=tt, psg=psg, jw=jw):
                    last = None
                    for fc in range(2):
                        for dt in range(NDT):
                            last = e.matmul(
                                psg[:, fc * 512:(fc + 1) * 512],
                                t.xlnT[:, dt * T + tt * 128: dt * T + (tt + 1) * 128],
                                t.wugS[jw][:, dt * 1024 + fc * 512: dt * 1024 + (fc + 1) * 512],
                                start=(dt == 0), stop=(dt == NDT - 1))
                    return last
                P.op("pe", w, gops, (s_pe, f"pf{ccg}"))
                if tt == NTT - 1:
                    st.pool_done[plw] = (s_pe, f"pf{ccg}")
                ccy = st.cc
                st.cc += 1
                psy = t.ps[ccy % 3]
                w = [(s_dve, f"hsbf{b}"), GW] + pair_waits(ccy)

                def yops(e, tt=tt, psy=psy):
                    last = None
                    for fc in range(2):
                        last = e.matmul(psy[:, fc * 512:(fc + 1) * 512],
                                        t.hs_bf[:, tt * 128:(tt + 1) * 128],
                                        t.wct_s[:, fc * 512:(fc + 1) * 512],
                                        start=True, stop=True)
                    return last
                P.op("pe", w, yops, (s_pe, f"pf{ccy}"))
                yp = tt % 2
                gw_ = [(s_pe, f"pf{ccg}")]
                if tt >= 1:
                    gw_.append((s_dve, f"ymul{b}t{tt-1}"))
                P.op("act", gw_, lambda e, psg=psg: e.activation(
                    t.sgb[:], psg[:], AF.Sigmoid), (s_act, f"gs{b}t{tt}"))
                st.pair_free[ccg] = (s_act, f"gs{b}t{tt}")
                yt = t.ostY[:, yp * 1024:(yp + 1) * 1024]
                yw = [(s_pe, f"pf{ccy}"), (s_act, f"gs{b}t{tt}")]
                if tt >= 2:
                    yw.append((s_act, f"sq2{b}t{tt-2}"))
                P.op("dve", yw, lambda e, psy=psy, yt=yt: e.tensor_mul(
                    yt, psy[:], t.sgb[:]), (s_dve, f"ymul{b}t{tt}"))
                st.pair_free[ccy] = (s_dve, f"ymul{b}t{tt}")
                dmp = [st.dump_prev[0]] if st.dump_prev[0] is not None else []
                P.op("act", [(s_dve, f"ymul{b}t{tt}")],
                     lambda e, yt=yt, tt=tt: e.activation(
                         t.dump[:], yt, AF.Square,
                         accum_out=t.sstat[:, tt * 2:tt * 2 + 1]),
                     (s_act, f"sq2{b}t{tt}"), fwaits=dmp)
                st.dump_prev[0] = (s_act, f"sq2{b}t{tt}")
            for tt in tts:
                P.op("act", [CST], lambda e, tt=tt: e.activation(
                    t.sstat[:, tt * 2 + 1:tt * 2 + 2],
                    t.sstat[:, tt * 2:tt * 2 + 1], AF.Ln,
                    bias=t.cb[:, CB_EPS_RMS:CB_EPS_RMS + 1],
                    scale=t.cb[:, CB_INVD:CB_INVD + 1]),
                    (s_act, f"r2a{b}t{tt}"), fwaits=[(s_act, f"sq2{b}t{tt}")])
            for tt in tts:
                P.op("act", [], lambda e, tt=tt: e.activation(
                    t.sstat[:, tt * 2 + 1:tt * 2 + 2],
                    t.sstat[:, tt * 2 + 1:tt * 2 + 2], AF.Exp,
                    scale=t.cb[:, CB_MHALF:CB_MHALF + 1]),
                    (s_act, f"rs2{b}t{tt}"), fwaits=[(s_act, f"r2a{b}t{tt}")])
            for tt in tts:
                def tsbop(e, tt=tt):
                    yp = tt % 2
                    ss = t.sstat[:, tt * 2:tt * 2 + 2]
                    yt = t.ostY[:, yp * 1024:(yp + 1) * 1024]
                    return e.scalar_tensor_tensor(t.tsb[:], yt, ss[:, 1:2],
                                                  t.cb[:, CB_RW:CB_RW + D],
                                                  ALU.mult, ALU.mult)
                tfw = [(s_dve, f"sadd{b}tt{tt-1}")] if tt > 0 else []
                P.op("dve", [(s_act, f"rs2{b}t{tt}"), CST], tsbop,
                     (s_dve, f"tsb{b}t{tt}"), fwaits=tfw)
                P.op("dve", [], lambda e, tt=tt: e.tensor_add(
                    t.xbuf[xp][:, tt * D:(tt + 1) * D],
                    t.xbuf[xp][:, tt * D:(tt + 1) * D], t.tsb[:]),
                    (s_dve, f"sadd{b}tt{tt}"), fwaits=[(s_dve, f"tsb{b}t{tt}")])
                ln_s1(b, 2, xp, tt)
        ln_s24(b, 2)

    xload(0)
    for b in range(NB):
        xp = b % 2
        if b == 0:
            for tt in range(NTT):
                ln_s1(0, 0, 0, tt)
            ln_s24(0, 0)
        ln_rest(b, 0, xp)
        ffn_pass(b, 0, xp,
                 tail_s1=lambda tt, b=b, xp=xp: ln_s1(b, 1, xp, tt),
                 tail_s24=lambda b=b: ln_s24(b, 1))
        ssm_phase(b, xp)
        ln_rest(b, 2, xp)
        if b + 1 < NB:
            xload(b + 1)
            ffn_pass(b, 1, xp,
                     tail_s1=lambda tt, b=b: ln_s1(b + 1, 0, (b + 1) % 2, tt),
                     tail_s24=lambda b=b: ln_s24(b + 1, 0))
        else:
            ffn_pass(b, 1, xp)
        stw = [(s_ga[tt], st.galast[(b, 1, tt)]) for tt in range(NTT)]
        P.op("sp", stw,
             lambda e, b=b, xp=xp: e.dma_start(
                 out=t.y_d[b * T:(b + 1) * T, :].rearrange("(tt p) d -> p tt d", p=128),
                 in_=t.xbuf[xp].ap().rearrange("p (tt d) -> p tt d", d=D)),
             (s_st, f"y{b}"))
    P.op("sp", [(s_st, f"y{NB-1}")], lambda e: e.nop(), None)


def build_program():
    nc = bass.Bass("TRN2", target_bir_lowering=False, debug=False,
                   detect_race_conditions=False)
    t = build_tensors(nc)
    emit(Px(None), t)
    with nc.Block() as blk:
        @blk.sync
        def _(eng):
            emit(Px("sp", eng), t)

        @blk.gpsimd
        def _(eng):
            emit(Px("gp", eng), t)

        @blk.tensor
        def _(eng):
            emit(Px("pe", eng), t)

        @blk.vector
        def _(eng):
            emit(Px("dve", eng), t)

        @blk.scalar
        def _(eng):
            emit(Px("act", eng), t)
    return nc


def build_cbias(router_bias, b_lambda, rms_weight):
    f32 = np.float32
    cb = np.zeros((128, CB_W), dtype=f32)
    cb[:, CB_RB:CB_RB + 16] = np.asarray(router_bias, f32)[None, :]
    cb[:, CB_BL:CB_BL + 16] = np.asarray(b_lambda, f32)[None, :]
    cb[:, CB_RW:CB_RW + D] = np.asarray(rms_weight, f32)[None, :]
    cb[:, CB_EPS_LN] = 1e-5
    cb[:, CB_EPS_RMS] = 1e-6
    cb[:, CB_MHALF] = -0.5
    cb[:, CB_INVD] = 1.0 / D
    cb[:, CB_IOTA] = np.arange(128, dtype=f32)
    rb8 = np.tile(np.asarray(router_bias, f32), NTT)
    cb[:, CB_RB8:CB_RB8 + NTT * 16] = rb8[None, :]
    bl = np.concatenate([np.asarray(b_lambda, f32), np.zeros(16, f32)])
    cb[:, CB_BL8:CB_BL8 + NTT * 32] = np.tile(bl, NTT)[None, :]
    cb[:, CB_C128:CB_C128 + NCH] = 128.0 * np.arange(NCH, dtype=f32)[None, :]
    cb[:, CB_ONES16:CB_ONES16 + 16] = 1.0
    return cb


def kernel(L0_attn_x, L0_attn_W_lambda, L0_attn_b_lambda, L0_attn_W_B,
           L0_attn_W_C, L0_attn_W_gate, L0_attn_rms_weight,
           L0_ffn_router_w, L0_ffn_router_bias, L0_ffn_up_proj,
           L0_ffn_gate_proj, L0_ffn_down_proj, L0_ffn_sh_gate,
           L0_ffn_sh_up, L0_ffn_sh_down):
    import ml_dtypes
    f32 = np.float32
    x = np.ascontiguousarray(np.asarray(L0_attn_x, dtype=f32))
    cb = build_cbias(L0_ffn_router_bias, L0_attn_b_lambda, L0_attn_rms_weight)
    toki = (np.arange(NTT, dtype=np.int32)[None, :] * 128
            + np.arange(128, dtype=np.int32)[:, None]).astype(np.int32)
    common = {
        "up": np.ascontiguousarray(np.asarray(L0_ffn_up_proj, f32)),
        "gate": np.ascontiguousarray(np.asarray(L0_ffn_gate_proj, f32)),
        "down": np.ascontiguousarray(np.asarray(L0_ffn_down_proj, f32)),
        "shu_t": np.ascontiguousarray(np.asarray(L0_ffn_sh_up, f32).T),
        "shg_t": np.ascontiguousarray(np.asarray(L0_ffn_sh_gate, f32).T),
        "shd_t": np.ascontiguousarray(np.asarray(L0_ffn_sh_down, f32).T),
        "wgate_t": np.ascontiguousarray(np.asarray(L0_attn_W_gate, f32).T),
        "router_t": np.ascontiguousarray(np.asarray(L0_ffn_router_w, f32).T),
        "wlamb_t": np.ascontiguousarray(np.concatenate(
            [np.asarray(L0_attn_W_lambda, f32).T,
             np.asarray(L0_attn_W_B, f32).T], axis=1)),
        "wct": np.ascontiguousarray(np.asarray(L0_attn_W_C, f32).T),
        "cbias": cb,
        "identb": np.eye(128, dtype=ml_dtypes.bfloat16),
        "identf": np.eye(128, dtype=f32),
        "toki": np.ascontiguousarray(toki),
    }
    in_maps = [dict(common, x=np.ascontiguousarray(x[c])) for c in range(8)]
    nc = build_program()
    res = run_bass_kernel_spmd(nc, in_maps, core_ids=list(range(8)))
    out = np.stack([np.asarray(res.results[c]["y"]) for c in range(8)], axis=0)
    return out.astype(np.float32, copy=False)


# version 14 (final)
# speedup vs baseline: 1.4128x; 1.4128x over previous
import numpy as np
import concourse.bass as bass
import concourse.mybir as mybir
from concourse.bass import IndirectOffsetOnAxis
from concourse.bass_utils import run_bass_kernel_spmd

F32 = mybir.dt.float32
BF16 = mybir.dt.bfloat16
I32 = mybir.dt.int32
AF = mybir.ActivationFunctionType
ALU = mybir.AluOpType
AX = mybir.AxisListType

B, S, D = 8, 8192, 1024
T = 1024
NB = S // T
NTT = T // 128
NDT = D // 128
EX = 512
NET = EX // 128
NRE = 16
NEXP = 18
NCH = 32
NSLOT = NCH * 128

CB_RB = 0
CB_BL = 16
CB_RW = 48
CB_EPS_LN = 1072
CB_EPS_RMS = 1073
CB_MHALF = 1074
CB_INVD = 1075
CB_IOTA = 1076
CB_C128 = 1080
CB_ONES16 = 1112
CB_W = 1128


class Sem:
    def __init__(self, nc, name, owner, step=1):
        self.name, self.owner, self.step = name, owner, step
        self.h = nc.alloc_semaphore(name)
        self.n = 0
        self.val = {}

    def record(self, ev):
        assert ev not in self.val, f"dup event {self.name}:{ev}"
        self.n += 1
        self.val[ev] = self.n * self.step

    def alias(self, new, old):
        self.val[new] = self.val[old]


class Px:

    def __init__(self, mode, eng=None):
        self.mode = mode
        self.eng = eng
        self.hwm = {}

    def op(self, engine, waits, fn, inc=None, fwaits=()):
        if self.mode is None:
            if inc is not None:
                inc[0].record(inc[1])
            return
        if self.mode != engine:
            return
        e = self.eng
        for sem, ev in waits:
            if sem.owner == engine:
                continue
            tgt = sem.val[ev]
            if self.hwm.get(sem.name, 0) >= tgt:
                continue
            e.wait_ge(sem.h, tgt)
            self.hwm[sem.name] = tgt
        for sem, ev in fwaits:
            tgt = sem.val[ev]
            if self.hwm.get(sem.name, 0) >= tgt:
                continue
            e.wait_ge(sem.h, tgt)
            self.hwm[sem.name] = tgt
        last = fn(e)
        if inc is not None:
            sem, ev = inc
            assert last is not None, f"inc {ev} with no instruction"
            last.then_inc(sem.h, sem.step)

    def alias(self, sem, new, old):
        if self.mode is None:
            sem.alias(new, old)


class NS:
    pass


def build_tensors(nc):
    t = NS()
    dt_ = nc.dram_tensor
    t.x_d = dt_("x", [S, D], F32, kind="ExternalInput").ap()
    t.y_d = dt_("y", [S, D], F32, kind="ExternalOutput").ap()
    t.up_d = dt_("up", [NRE, D, EX], F32, kind="ExternalInput").ap()
    t.gate_d = dt_("gate", [NRE, D, EX], F32, kind="ExternalInput").ap()
    t.down_d = dt_("down", [NRE, EX, D], F32, kind="ExternalInput").ap()
    t.shu_t = dt_("shu_t", [D, D], F32, kind="ExternalInput").ap()
    t.shg_t = dt_("shg_t", [D, D], F32, kind="ExternalInput").ap()
    t.shd_t = dt_("shd_t", [D, D], F32, kind="ExternalInput").ap()
    t.wgate_t = dt_("wgate_t", [D, D], F32, kind="ExternalInput").ap()
    t.router_t = dt_("router_t", [D, 16], F32, kind="ExternalInput").ap()
    t.wlamb_t = dt_("wlamb_t", [D, 32], F32, kind="ExternalInput").ap()
    t.wct_d = dt_("wct", [16, D], F32, kind="ExternalInput").ap()
    t.cbias_d = dt_("cbias", [128, CB_W], F32, kind="ExternalInput").ap()
    t.toki_d = dt_("toki", [128, NTT], I32, kind="ExternalInput").ap()
    t.identb_d = dt_("identb", [128, 128], BF16, kind="ExternalInput").ap()
    t.identf_d = dt_("identf", [128, 128], F32, kind="ExternalInput").ap()
    t.wexp2 = dt_("wexp2", [NEXP, 128, NDT * 1024], BF16, kind="Internal").ap()
    t.wdn2 = dt_("wdn2", [NEXP, 128, NET * 1024], BF16, kind="Internal").ap()
    t.wgate2 = dt_("wgate2", [128, NDT * 1024], BF16, kind="Internal").ap()
    t.xlnb_d = dt_("xlnb_d", [T, D], BF16, kind="Internal").ap()
    t.tokmap_d = dt_("tokmap_d", [NSLOT, 1], I32, kind="Internal").ap()
    t.ws_d = dt_("ws_d", [NSLOT, 1], F32, kind="Internal").ap()
    t.out_d = dt_("out_d", [NSLOT, D], F32, kind="Internal").ap()

    sb = nc.alloc_sbuf_tensor
    t.xbuf = [sb(f"xbuf{i}", [128, NTT * D], F32) for i in range(2)]
    t.xlnT = sb("xlnT", [128, NDT * T], BF16)
    t.wugS = [sb(f"wugS{i}", [128, NDT * 1024], BF16) for i in range(3)]
    t.wdnS = [sb(f"wdnS{i}", [128, NET * 1024], BF16) for i in range(2)]
    t.hidden = sb("hidden", [128, NET * T], BF16)
    t.sgb = sb("sgb", [128, 1024], F32)
    t.ostY = sb("ostY", [128, 2048], F32)
    t.dump = sb("dump", [128, 1024], BF16)
    t.xln_tile = [sb(f"xlnt{i}", [128, 1024], BF16) for i in range(2)]
    t.cb = sb("cb", [128, CB_W], F32)
    t.identb = sb("identb_s", [128, 128], BF16)
    t.identf = sb("identf_s", [128, 128], F32)
    t.toki_s = sb("toki_s", [128, NTT], I32)
    t.tokmap_i = [sb(f"tokmap{i}", [128, 1], I32) for i in range(2)]
    t.rw_s = sb("rw_s", [128, NDT * 16], BF16)
    t.lb_s = sb("lb_s", [128, NDT * 32], BF16)
    t.wct_s = sb("wct_s", [16, D], BF16)
    t.rT = sb("rT", [16, T], F32)
    t.zlamT = t.ostY[0:16, 0:T]
    t.zbxT = sb("zbxT", [16, T], F32)
    t.hs_bf = sb("hs_bf", [16, T], BF16)
    t.hcarry = sb("hcarry", [16, 1], F32)
    t.mT = sb("mT", [16, T], BF16)
    t.onesT = sb("onesT", [16, T], BF16)
    t.ones16 = sb("ones16", [16, 1], F32)
    t.onesr = sb("onesr", [1, 128], F32)
    t.zrt = sb("zrt", [128, NTT * 16], F32)
    t.top8 = sb("top8", [128, NTT * 8], F32)
    t.rsc = sb("rsc", [128, NTT * 4], F32)
    t.mask1a = sb("mask1a", [128, NTT * 16], F32)
    t.mask2a = sb("mask2a", [128, NTT * 16], F32)
    t.msk_t = sb("msk_t", [128, 16], F32)
    t.ch1 = sb("ch1", [128, 16], F32)
    t.stats = sb("stats", [128, NTT * 8], F32)
    t.sstat = sb("sstat", [128, NTT * 2], F32)
    t.zlb = sb("zlb", [128, NTT * 32], F32)
    t.totc = sb("totc", [16, 2], F32)
    t.posrow = sb("posrow", [1, 80], F32)
    t.padi = sb("padi", [1, 16], I32)
    t.padc = sb("padc", [1, 2], I32)
    t.pos16 = sb("pos16", [16, 1], F32)
    t.posb2 = sb("posb2", [128, 32], F32)
    t.pos2b = sb("pos2b", [128, 16], F32)
    t.cmpb = sb("cmpb", [16, 32], F32)
    t.rnkb = sb("rnkb", [128, 16], F32)
    t.basebuf = sb("basebuf", [128, 16], F32)
    t.slotf = sb("slotf", [128, 16], F32)
    t.sloti = sb("sloti", [128, 16], I32)
    t.idxwf = sb("idxwf", [128, NCH], F32)
    t.idxw_i = sb("idxw_i", [128, NCH], I32)
    t.xc = [sb(f"xc{i}", [128, 1024], BF16) for i in range(2)]
    t.xcT = [sb(f"xcT{i}", [128, 1024], BF16) for i in range(2)]
    t.hc = [sb(f"hc{i}", [128, 512], BF16) for i in range(2)]
    t.wsc = [sb(f"wsc{i}", [128, 1], F32) for i in range(2)]

    pt = nc.alloc_psum_tensor
    t.ps = [pt(f"ps{i}", [128, 1024], F32) for i in range(3)]
    t.psTR = pt("psTR", [128, 1024], BF16)
    t.psTR2 = pt("psTR2", [128, 1024], BF16)

    t.s_x = Sem(nc, "s_x", "dma_sp", 16)
    t.s_cst = Sem(nc, "s_cst", "dma_sp", 16)
    t.s_st = Sem(nc, "s_st", "dma_sp", 16)
    t.s_gw = Sem(nc, "s_gw", "dma_gp", 16)
    t.s_cv = [Sem(nc, f"s_cv{k}", "dma_gp", 16) for k in range(NEXP)]
    t.s_cvw = Sem(nc, "s_cvw", "dma_gp", 16)
    t.s_zz = Sem(nc, "s_zz", "dma_gp", 16)
    t.s_phu = [Sem(nc, f"s_phu{j}", "dma_sp", 16) for j in range(3)]
    t.s_phd = [Sem(nc, f"s_phd{j}", "dma_sp", 16) for j in range(2)]
    t.s_pgu = [Sem(nc, f"s_pgu{j}", "dma_gp", 16) for j in range(3)]
    t.s_pgd = [Sem(nc, f"s_pgd{j}", "dma_gp", 16) for j in range(2)]
    t.s_tw = [Sem(nc, f"s_tw{j}", "dma_sp", 16) for j in range(2)]
    t.s_xg = [Sem(nc, f"s_xg{j}", "dma_gp", 16) for j in range(2)]
    t.s_sc = Sem(nc, "s_sc", "dma_gp", 16)
    t.s_os = [Sem(nc, f"s_os{j}", "dma_sp", 16) for j in range(2)]
    t.s_xl = [Sem(nc, f"s_xl{j}", "dma_sp", 16) for j in range(2)]
    t.s_ga = [Sem(nc, f"s_ga{j}", "dma_gp", 16) for j in range(8)]
    t.s_pe = Sem(nc, "s_pe", "pe", 1)
    t.s_dve = Sem(nc, "s_dve", "dve", 1)
    t.s_act = Sem(nc, "s_act", "act", 1)
    return t


def conv_srcs(t, k):
    if k < NRE:
        return t.up_d[k], t.gate_d[k], t.down_d[k]
    j = k - NRE
    return (t.shu_t[:, j * 512:(j + 1) * 512],
            t.shg_t[:, j * 512:(j + 1) * 512],
            t.shd_t[j * 512:(j + 1) * 512, :])


def emit(P, t):
    s_x, s_st, s_cst, s_gw = t.s_x, t.s_st, t.s_cst, t.s_gw
    s_cv, s_cvw, s_zz = t.s_cv, t.s_cvw, t.s_zz
    s_phu, s_phd, s_pgu, s_pgd = t.s_phu, t.s_phd, t.s_pgu, t.s_pgd
    s_tw, s_xg, s_sc, s_os, s_xl, s_ga = (t.s_tw, t.s_xg, t.s_sc, t.s_os,
                                          t.s_xl, t.s_ga)
    s_pe, s_dve, s_act = t.s_pe, t.s_dve, t.s_act
    wexp2f = t.wexp2.rearrange("k p c -> (k p) c")
    wdn2f = t.wdn2.rearrange("k p c -> (k p) c")

    P.op("sp", [], lambda e: e.dma_start(out=t.cb[:], in_=t.cbias_d), (s_cst, "cb"))
    P.op("sp", [], lambda e: e.dma_start(out=t.identb[:], in_=t.identb_d), (s_cst, "idb"))
    P.op("sp", [], lambda e: e.dma_start(out=t.identf[:], in_=t.identf_d), (s_cst, "idf"))
    P.op("sp", [], lambda e: e.dma_start(out=t.toki_s[:], in_=t.toki_d), (s_cst, "tok"))
    CST = (s_cst, "tok")
    P.op("gp", [], lambda e: e.dma_start(
        out=t.rw_s.ap().rearrange("p (dt n) -> p dt n", n=16),
        in_=t.router_t.rearrange("(dt p) n -> p dt n", p=128)), (s_gw, "rw"))
    P.op("gp", [], lambda e: e.dma_start(
        out=t.lb_s.ap().rearrange("p (dt n) -> p dt n", n=32),
        in_=t.wlamb_t.rearrange("(dt p) n -> p dt n", p=128)), (s_gw, "lb"))
    P.op("gp", [], lambda e: e.dma_start(out=t.wct_s[:], in_=t.wct_d), (s_gw, "wct"))
    GW = (s_gw, "wct")
    for k in [16, 17] + list(range(NRE)):
        us, gs, ds = conv_srcs(t, k)
        d3 = t.wexp2[k].rearrange("p (dt c) -> p dt c", c=1024)
        P.op("gp", [], lambda e, k=k, us=us, d3=d3: e.dma_start(
            out=d3[:, :, 0:512], in_=us.rearrange("(dt p) c -> p dt c", p=128)),
            (s_cv[k], f"cvu{k}"))
        P.op("gp", [], lambda e, k=k, gs=gs, d3=d3: e.dma_start(
            out=d3[:, :, 512:1024], in_=gs.rearrange("(dt p) c -> p dt c", p=128)),
            (s_cv[k], f"cvg{k}"))
        P.op("gp", [], lambda e, k=k, ds=ds: e.dma_start(
            out=t.wdn2[k].rearrange("p (et c) -> p et c", c=1024),
            in_=ds.rearrange("(et p) c -> p et c", p=128)),
            (s_cv[k], f"cvd{k}"))
    P.op("gp", [], lambda e: e.dma_start(
        out=t.wgate2.rearrange("p (dt c) -> p dt c", c=1024),
        in_=t.wgate_t.rearrange("(dt p) c -> p dt c", p=128)), (s_cvw, "cwg"))
    def init_dve(e):
        e.memset(t.hcarry[:], 0.0)
        e.memset(t.onesT[:], 1.0)
        e.memset(t.ones16[:], 1.0)
        e.memset(t.onesr[:], 1.0)
        e.memset(t.idxwf[:], 0.0)
        e.memset(t.padc[0:1, 0:1], 127)
        e.memset(t.padc[0:1, 1:2], -128)
        return e.memset(t.idxw_i[:], 0)
    P.op("dve", [], init_dve, (s_dve, "init"))
    P.op("gp", [(s_dve, "init")], lambda e: e.dma_start(
        out=t.tokmap_d.rearrange("(p c) w -> p (c w)", p=128), in_=t.idxw_i[:]),
        (s_zz, "zza"))
    P.op("gp", [], lambda e: e.dma_start(
        out=t.ws_d.rearrange("(p c) w -> p (c w)", p=128), in_=t.idxwf[:]),
        (s_zz, "zz"))

    st = NS()
    st.cc = 0
    st.dump_prev = [None]
    st.pair_free = {}
    st.pl = 0
    st.pool_evu = {}
    st.pool_evd = {}
    st.sgfree = {}
    st.hcfree = {}
    st.xcev = {}
    st.xcTfree = {}
    st.osev = {}
    st.twev = {}
    st.xgev = {}
    st.galast = {}
    st.os_store = {}
    st.ga_prev = []
    st.wsl_prev = []
    st.xg_prev = []
    st.scat_prev = []
    st.ctr_prev = None

    def pair_waits(cc):
        return [st.pair_free[cc - 3]] if cc - 3 in st.pair_free else []

    def xload(b):
        xp = b % 2
        waits = [(s_st, f"y{b-2}")] if b >= 2 else []
        P.op("sp", waits, lambda e, b=b, xp=xp: e.dma_start(
            out=t.xbuf[xp].ap().rearrange("p (tt d) -> p tt d", d=D),
            in_=t.x_d[b * T:(b + 1) * T, :].rearrange("(tt p) d -> p tt d", p=128)),
            (s_x, f"x{b}"))

    def ugd(pl):
        return (s_pe, f"ugd_u{pl}")

    def dnd(pl):
        return (s_pe, f"dnd_u{pl}")

    def pool_load_regular(k, wdn=True):
        pl = st.pl
        st.pl += 1
        ju, jd = pl % 3, pl % 2
        w = [(s_cv[k] if k < NEXP else s_cvw, f"cvd{k}" if k < NEXP else "cwg")]
        if pl >= 3:
            w.append(ugd(pl - 3))
        if k < NEXP:
            src = t.wexp2[k]
        else:
            src = t.wgate2
        P.op("sp", w, lambda e, ju=ju, src=src: e.dma_start(
            out=t.wugS[ju][:], in_=src), (s_phu[ju], f"phu{pl}"))
        st.pool_evu[pl] = (s_phu[ju], f"phu{pl}")
        if wdn:
            wd = [(s_cv[k], f"cvd{k}")]
            if pl >= 2:
                wd.append(dnd(pl - 2))
            P.op("sp", wd, lambda e, jd=jd, k=k: e.dma_start(
                out=t.wdnS[jd][:], in_=t.wdn2[k]), (s_phd[jd], f"phd{pl}"))
            st.pool_evd[pl] = (s_phd[jd], f"phd{pl}")
        return pl, ju, jd

    def chunk_load_wug(pl, c, idx_ev):
        ju = pl % 3
        w = [idx_ev]
        if pl >= 3:
            w.append(ugd(pl - 3))
        w += [(s_cv[k], f"cvd{k}") for k in range(NRE)]
        P.op("gp", w, lambda e, ju=ju, c=c: e.indirect_dma_start(
            t.wugS[ju][:], None, wexp2f,
            IndirectOffsetOnAxis(t.idxw_i[:, c:c + 1], 0)),
            (s_pgu[ju], f"pgu{pl}"))
        st.pool_evu[pl] = (s_pgu[ju], f"pgu{pl}")

    def chunk_load_wdn(pl, c, idx_ev):
        jd = pl % 2
        w = [idx_ev]
        if pl >= 2:
            w.append(dnd(pl - 2))
        w += [(s_cv[k], f"cvd{k}") for k in range(NRE)]
        P.op("gp", w, lambda e, jd=jd, c=c: e.indirect_dma_start(
            t.wdnS[jd][:], None, wdn2f,
            IndirectOffsetOnAxis(t.idxw_i[:, c:c + 1], 0)),
            (s_pgd[jd], f"pgd{pl}"))
        st.pool_evd[pl] = (s_pgd[jd], f"pgd{pl}")

    def _xw_of(b, ph, tt):
        if ph == 0:
            if b == 0:
                return [(s_x, "x0")]
            return [(s_x, f"x{b}")]
        if ph == 1:
            return [(s_ga[tt], st.galast[(b, 0, tt)])]
        return [(s_dve, f"sadd{b}tt{tt}")]

    def _sc(tt):
        return t.stats[:, tt * 8:tt * 8 + 8]

    def ln_s1(b, ph, xp, tt):
        pre = f"{b}p{ph}"
        xw = _xw_of(b, ph, tt)
        x_ap = t.xbuf[xp][:, tt * D:(tt + 1) * D]
        P.op("dve", xw, lambda e: e.reduce_sum(
            _sc(tt)[:, 0:1], x_ap, axis=AX.X), (s_dve, f"lns{pre}t{tt}"))
        dmp = [st.dump_prev[0]] if st.dump_prev[0] is not None else []
        P.op("act", xw, lambda e: e.activation(
            t.dump[:], x_ap, AF.Square, accum_out=_sc(tt)[:, 1:2]),
            (s_act, f"lnq{pre}t{tt}"), fwaits=dmp)
        st.dump_prev[0] = (s_act, f"lnq{pre}t{tt}")
        P.op("act", [(s_dve, f"lns{pre}t{tt}"), CST],
             lambda e: e.activation(
                 _sc(tt)[:, 6:7], _sc(tt)[:, 0:1], AF.Square,
                 scale=t.cb[:, CB_INVD:CB_INVD + 1]),
             (s_act, f"lnq2{pre}t{tt}"))

    def ln_s24(b, ph):
        pre = f"{b}p{ph}"
        for tt in range(NTT):
            P.op("dve", [(s_act, f"lnq{pre}t{tt}"), (s_act, f"lnq2{pre}t{tt}")],
                 lambda e, tt=tt: e.scalar_tensor_tensor(
                     _sc(tt)[:, 3:4], _sc(tt)[:, 1:2], 1.0 / D,
                     _sc(tt)[:, 6:7], ALU.mult, ALU.subtract),
                 (s_dve, f"lnv{pre}t{tt}"))
        for tt in range(NTT):
            P.op("act", [(s_dve, f"lnv{pre}t{tt}"), CST],
                 lambda e, tt=tt: e.activation(
                     _sc(tt)[:, 4:5], _sc(tt)[:, 3:4], AF.Ln,
                     bias=t.cb[:, CB_EPS_LN:CB_EPS_LN + 1]),
                 (s_act, f"lnr1{pre}t{tt}"))
        for tt in range(NTT):
            P.op("act", [], lambda e, tt=tt: e.activation(
                _sc(tt)[:, 4:5], _sc(tt)[:, 4:5], AF.Exp,
                scale=t.cb[:, CB_MHALF:CB_MHALF + 1]),
                (s_act, f"lnr{pre}t{tt}"), fwaits=[(s_act, f"lnr1{pre}t{tt}")])
        for tt in range(NTT):
            P.op("dve", [(s_act, f"lnr{pre}t{tt}")],
                 lambda e, tt=tt: e.scalar_tensor_tensor(
                     _sc(tt)[:, 5:6], _sc(tt)[:, 0:1], -1.0 / D,
                     _sc(tt)[:, 4:5], ALU.mult, ALU.mult),
                 (s_dve, f"lnm{pre}t{tt}"))

    def ln_rest(b, ph, xp):
        pre = f"{b}p{ph}"

        def x_ap(tt):
            return t.xbuf[xp][:, tt * D:(tt + 1) * D]

        def norm_op(tt):
            nw = [(s_dve, f"lnm{pre}t{tt}")]
            if tt >= 2:
                nw.append((s_pe, f"tr{pre}t{tt-2}"))
                if ph != 1:
                    nw.append((s_xl[tt % 2], f"xl{pre}t{tt-2}"))
            P.op("act", nw, lambda e, tt=tt: e.activation(
                t.xln_tile[tt % 2][:], x_ap(tt), AF.Identity,
                bias=_sc(tt)[:, 5:6], scale=_sc(tt)[:, 4:5]),
                (s_act, f"lnn{pre}t{tt}"), fwaits=[(s_act, f"lnr{pre}t{tt}")])
            if ph != 1:
                xw2 = [(s_act, f"lnn{pre}t{tt}")]
                if tt == 0:
                    xw2 += list(st.xg_prev)
                P.op("sp", xw2, lambda e, tt=tt: e.dma_start(
                    out=t.xlnb_d[tt * 128:(tt + 1) * 128, :],
                    in_=t.xln_tile[tt % 2][:]),
                     (s_xl[tt % 2], f"xl{pre}t{tt}"))

        def trcopy_op(tt):
            ptr = t.psTR if tt % 2 == 0 else t.psTR2
            P.op("act", [(s_pe, f"tr{pre}t{tt}")],
                 lambda e, tt=tt, ptr=ptr: e.copy(
                     t.xlnT.ap().rearrange("p (dt c) -> p dt c", c=T)
                     [:, :, tt * 128:(tt + 1) * 128],
                     ptr.ap().rearrange("p (dt c) -> p dt c", c=128)),
                 (s_act, f"trc{pre}t{tt}"))

        for tt in range(NTT):
            norm_op(tt)
            if tt >= 1:
                trcopy_op(tt - 1)
        trcopy_op(NTT - 1)
        for tt in range(NTT):
            ptr = t.psTR if tt % 2 == 0 else t.psTR2
            trw = [(s_act, f"lnn{pre}t{tt}"), CST]
            if tt >= 2:
                trw.append((s_act, f"trc{pre}t{tt-2}"))

            def trops(e, tt=tt, ptr=ptr):
                last = None
                for dt in range(NDT):
                    last = e.transpose(ptr[:, dt * 128:(dt + 1) * 128],
                                       t.xln_tile[tt % 2][:, dt * 128:(dt + 1) * 128],
                                       t.identb[:])
                return last
            P.op("pe", trw, trops, (s_pe, f"tr{pre}t{tt}"))

        if ph != 1:
            router(b, ph)
            idx_build(b, ph)

    def router(b, ph):
        pre = f"{b}p{ph}"
        cc = st.cc
        st.cc += 1
        ps = t.ps[cc % 3]
        for tt in range(NTT):
            w = [(s_act, f"trc{pre}t{tt}"), GW]
            if tt == 0:
                w += pair_waits(cc)

            def rops(e, tt=tt, ps=ps):
                last = None
                for dt in range(NDT):
                    last = e.matmul(ps[:, tt * 16:(tt + 1) * 16],
                                    t.xlnT[:, dt * T + tt * 128: dt * T + (tt + 1) * 128],
                                    t.rw_s[:, dt * 16:(dt + 1) * 16],
                                    start=(dt == 0), stop=(dt == NDT - 1))
                return last
            P.op("pe", w, rops,
                 (s_pe, f"pf{cc}") if tt == NTT - 1 else None)
        def rdrain(e, ps=ps):
            last = None
            for tt in range(NTT):
                last = e.tensor_add(t.zrt[:, tt * 16:(tt + 1) * 16],
                                    ps[:, tt * 16:(tt + 1) * 16],
                                    t.cb[:, CB_RB:CB_RB + 16])
            return last
        P.op("dve", [(s_pe, f"pf{cc}"), CST], rdrain, (s_dve, f"pd{cc}"))
        st.pair_free[cc] = (s_dve, f"pd{cc}")

        def sl(buf, tt, wdt):
            return buf[:, tt * wdt:(tt + 1) * wdt]

        for tt in range(NTT):
            P.op("dve", [], lambda e, tt=tt: e.max(sl(t.top8, tt, 8),
                                                   sl(t.zrt, tt, 16)),
                 (s_dve, f"rmx{pre}t{tt}"), fwaits=[(s_dve, f"pd{cc}")])
        for tt in range(NTT):
            P.op("dve", [], lambda e, tt=tt: e.tensor_sub(
                sl(t.rsc, tt, 4)[:, 0:1], sl(t.top8, tt, 8)[:, 0:1],
                sl(t.top8, tt, 8)[:, 1:2]),
                (s_dve, f"d12{pre}t{tt}"), fwaits=[(s_dve, f"rmx{pre}t{tt}")])
        for tt in range(NTT):
            P.op("act", [(s_dve, f"d12{pre}t{tt}")],
                 lambda e, tt=tt: e.activation(sl(t.rsc, tt, 4)[:, 1:2],
                                               sl(t.rsc, tt, 4)[:, 0:1],
                                               AF.Sigmoid),
                 (s_act, f"s1{pre}t{tt}"))
        for tt in range(NTT):
            def mraw(e, tt=tt):
                rs = sl(t.rsc, tt, 4)
                e.tensor_scalar_mul(rs[:, 2:3], rs[:, 1:2], 0.5)
                e.tensor_scalar(rs[:, 3:4], rs[:, 1:2], -0.5, 0.5,
                                ALU.mult, ALU.add)
                e.tensor_scalar(sl(t.mask1a, tt, 16), sl(t.zrt, tt, 16),
                                sl(t.top8, tt, 8)[:, 0:1], None, ALU.is_equal)
                return e.tensor_scalar(sl(t.mask2a, tt, 16), sl(t.zrt, tt, 16),
                                       sl(t.top8, tt, 8)[:, 1:2], None,
                                       ALU.is_equal)
            P.op("dve", [(s_act, f"s1{pre}t{tt}")], mraw,
                 (s_dve, f"mr{pre}t{tt}") if tt == NTT - 1 else None)
        prevs = (s_dve, f"mr{pre}t{NTT-1}")
        for tt in range(NTT):
            P.op("dve", [], lambda e, tt=tt: e.tensor_tensor_scan(
                t.ch1[:], t.cb[:, CB_ONES16:CB_ONES16 + 16],
                sl(t.mask1a, tt, 16), 0.0, ALU.mult, ALU.add),
                (s_dve, f"saS{pre}t{tt}"), fwaits=[prevs])
            P.op("dve", [], lambda e: e.tensor_scalar(
                t.msk_t[:], t.ch1[:], 1.0, None, ALU.is_equal),
                (s_dve, f"saE{pre}t{tt}"), fwaits=[(s_dve, f"saS{pre}t{tt}")])
            P.op("dve", [], lambda e, tt=tt: e.tensor_mul(
                sl(t.mask1a, tt, 16), sl(t.mask1a, tt, 16), t.msk_t[:]),
                (s_dve, f"sa{pre}t{tt}"), fwaits=[(s_dve, f"saE{pre}t{tt}")])
            prevs = (s_dve, f"sa{pre}t{tt}")
        prevs = (s_dve, f"sa{pre}t{NTT-1}")
        for tt in range(NTT):
            P.op("dve", [], lambda e, tt=tt: e.tensor_mul(
                t.msk_t[:], sl(t.mask1a, tt, 16), sl(t.mask2a, tt, 16)),
                (s_dve, f"sbA{pre}t{tt}"), fwaits=[prevs])
            P.op("dve", [], lambda e, tt=tt: e.tensor_sub(
                sl(t.mask2a, tt, 16), sl(t.mask2a, tt, 16), t.msk_t[:]),
                (s_dve, f"sbB{pre}t{tt}"), fwaits=[(s_dve, f"sbA{pre}t{tt}")])
            P.op("dve", [], lambda e, tt=tt: e.tensor_tensor_scan(
                t.ch1[:], t.cb[:, CB_ONES16:CB_ONES16 + 16],
                sl(t.mask2a, tt, 16), 0.0, ALU.mult, ALU.add),
                (s_dve, f"sbC{pre}t{tt}"), fwaits=[(s_dve, f"sbB{pre}t{tt}")])
            P.op("dve", [], lambda e: e.tensor_scalar(
                t.msk_t[:], t.ch1[:], 1.0, None, ALU.is_equal),
                (s_dve, f"sbD{pre}t{tt}"), fwaits=[(s_dve, f"sbC{pre}t{tt}")])
            P.op("dve", [], lambda e, tt=tt: e.tensor_mul(
                sl(t.mask2a, tt, 16), sl(t.mask2a, tt, 16), t.msk_t[:]),
                (s_dve, f"sb{pre}t{tt}"), fwaits=[(s_dve, f"sbD{pre}t{tt}")])
            prevs = (s_dve, f"sb{pre}t{tt}")

    def idx_build(b, ph):
        pre = f"{b}p{ph}"
        p = 0 if ph == 0 else 1

        def mtr_chunk(masks, tag, sanev, extra=()):
            cc = st.cc
            st.cc += 1
            ps = t.ps[cc % 3]

            def trs(e, masks=masks, ps=ps):
                last = None
                for tt in range(NTT):
                    last = e.transpose(ps[0:16, tt * 128:(tt + 1) * 128],
                                       masks[:, tt * 16:(tt + 1) * 16],
                                       t.identf[:])
                return last
            P.op("pe", [sanev, CST] + list(extra) + pair_waits(cc), trs,
                 (s_pe, f"pf{cc}"))
            P.op("act", [(s_pe, f"pf{cc}")],
                 lambda e, ps=ps: e.copy(t.mT[:], ps[0:16, 0:T]),
                 (s_act, f"pd{cc}"))
            st.pair_free[cc] = (s_act, f"pd{cc}")
            return (s_act, f"pd{cc}")

        def one_list(li, masks, sanev):
            if li == 1:
                mtev = mtr_chunk(masks, "m1",
                                 sanev, extra=[(s_dve, f"pr0{pre}")])
            else:
                mtev = mtr_chunk(masks, "m0", sanev)
            P.op("dve", [mtev], lambda e: e.tensor_tensor_scan(
                t.rT[:], t.onesT[:], t.mT[:], 0.0, ALU.mult, ALU.add),
                (s_dve, f"rs{li}{pre}"))
            P.op("dve", [], lambda e, li=li: e.tensor_copy(
                t.totc[:, li:li + 1], t.rT[:, T - 1:T]),
                (s_dve, f"tc{li}{pre}"), fwaits=[(s_dve, f"rs{li}{pre}")])
            P.op("dve", [], lambda e: e.tensor_mul(t.rT[:], t.rT[:], t.mT.ap()),
                 (s_dve, f"pr{li}{pre}"), fwaits=[(s_dve, f"tc{li}{pre}")])
            cc = st.cc
            st.cc += 1
            ps = t.ps[cc % 3]

            def rnk(e, li=li, ps=ps):
                last = None
                for tt in range(NTT):
                    last = e.matmul(ps[:, li * 8 + tt:li * 8 + tt + 1],
                                    t.rT[:, tt * 128:(tt + 1) * 128],
                                    t.ones16[:], start=True, stop=True)
                return last
            P.op("pe", [(s_dve, f"pr{li}{pre}")] + pair_waits(cc), rnk,
                 (s_pe, f"pf{cc}"))
            P.op("dve", [(s_pe, f"pf{cc}")],
                 lambda e, li=li, ps=ps: e.tensor_scalar(
                     t.rnkb[:, li * 8:(li + 1) * 8], ps[:, li * 8:(li + 1) * 8],
                     1.0, -1.0, ALU.mult, ALU.add),
                 (s_dve, f"pd{cc}"))
            st.pair_free[cc] = (s_dve, f"pd{cc}")
            return (s_dve, f"pd{cc}")

        rk1 = one_list(0, t.mask1a, (s_dve, f"sa{pre}t{NTT-1}"))
        rk2 = one_list(1, t.mask2a, (s_dve, f"sb{pre}t{NTT-1}"))
        P.op("dve", [], lambda e: e.tensor_add(
            t.totc[:, 1:2], t.totc[:, 0:1], t.totc[:, 1:2]),
            (s_dve, f"tt{pre}"), fwaits=[(s_dve, f"tc1{pre}")])
        cc = st.cc
        st.cc += 1
        ps = t.ps[cc % 3]

        def tottr(e, ps=ps):
            e.transpose(ps[0:1, 16:32], t.totc[:, 0:1], t.identf[0:16, 0:16])
            return e.transpose(ps[0:1, 32:48], t.totc[:, 1:2],
                               t.identf[0:16, 0:16])
        P.op("pe", [(s_dve, f"tt{pre}")] + pair_waits(cc), tottr, (s_pe, f"pf{cc}"))
        P.op("act", [(s_pe, f"pf{cc}")],
             lambda e, ps=ps: e.copy(t.posrow[0:1, 16:48], ps[0:1, 16:48]),
             (s_act, f"pd{cc}"))
        st.pair_free[cc] = (s_act, f"pd{cc}")
        totr_ev = (s_act, f"pd{cc}")
        pr = t.posrow

        P.op("dve", [totr_ev], lambda e: e.tensor_scalar(
            pr[0:1, 48:64], pr[0:1, 32:48], 127.0, None, ALU.add),
            (s_dve, f"pm0{pre}"))
        P.op("dve", [], lambda e: e.tensor_copy(
            t.padi[:], pr[0:1, 48:64]), (s_dve, f"pm1{pre}"),
            fwaits=[(s_dve, f"pm0{pre}")])
        P.op("dve", [], lambda e: e.tensor_scalar(
            t.padi[:], t.padi[:], t.padc[0:1, 1:2], None, ALU.bitwise_and),
            (s_dve, f"pm2{pre}"), fwaits=[(s_dve, f"pm1{pre}")])
        P.op("dve", [], lambda e: e.tensor_copy(
            pr[0:1, 48:64], t.padi[:]),
            (s_dve, f"pm4{pre}"), fwaits=[(s_dve, f"pm2{pre}")])
        P.op("dve", [], lambda e: e.tensor_tensor_scan(
            pr[0:1, 64:80], t.onesr[:, 0:16], pr[0:1, 48:64], 0.0,
            ALU.mult, ALU.add),
            (s_dve, f"pm5{pre}"), fwaits=[(s_dve, f"pm4{pre}")])
        P.op("dve", [], lambda e: e.tensor_sub(
            pr[0:1, 0:16], pr[0:1, 64:80], pr[0:1, 48:64]),
            (s_dve, f"pos{pre}"), fwaits=[(s_dve, f"pm5{pre}")])
        cc = st.cc
        st.cc += 1
        ps = t.ps[cc % 3]
        P.op("pe", [(s_dve, f"pos{pre}")] + pair_waits(cc),
             lambda e, ps=ps: e.transpose(ps[0:16, 64:65], pr[0:1, 0:16],
                                          t.identf[0:1, 0:1]),
             (s_pe, f"pf{cc}"))
        P.op("act", [(s_pe, f"pf{cc}")],
             lambda e, ps=ps: e.copy(t.pos16[:], ps[0:16, 64:65]),
             (s_act, f"pd{cc}"))
        st.pair_free[cc] = (s_act, f"pd{cc}")
        pos16_ev = (s_act, f"pd{cc}")
        cc = st.cc
        st.cc += 1
        ps = t.ps[cc % 3]
        P.op("pe", [(s_dve, f"pos{pre}")] + pair_waits(cc),
             lambda e, ps=ps: e.matmul(ps[:, 0:32], t.onesr[:],
                                       pr[0:1, 0:32], start=True, stop=True),
             (s_pe, f"pf{cc}"))
        P.op("dve", [(s_pe, f"pf{cc}")],
             lambda e, ps=ps: e.tensor_copy(t.posb2[:], ps[:, 0:32]),
             (s_dve, f"pd{cc}"))
        st.pair_free[cc] = (s_dve, f"pd{cc}")
        P.op("dve", [], lambda e: e.tensor_add(
            t.pos2b[:], t.posb2[:, 0:16], t.posb2[:, 16:32]),
            (s_dve, f"p2b{pre}"), fwaits=[(s_dve, f"pd{cc}")])
        for tt in range(NTT):
            for li in range(2):
                masks = t.mask1a if li == 0 else t.mask2a
                posb = t.posb2[:, 0:16] if li == 0 else t.pos2b[:]
                g = tt * 2 + li
                P.op("dve", [], lambda e, tt=tt, masks=masks, posb=posb:
                     e.tensor_mul(t.msk_t[:], masks[:, tt * 16:(tt + 1) * 16],
                                  posb),
                     (s_dve, f"bm{g}{pre}"),
                     fwaits=([(s_dve, f"p2b{pre}")] if g == 0 else
                             [(s_dve, f"bs{g-1}{pre}")]))
                P.op("dve", [], lambda e, g=g: e.reduce_sum(
                    t.basebuf[:, g:g + 1], t.msk_t[:], axis=AX.X),
                    (s_dve, f"bs{g}{pre}"),
                    fwaits=[(s_dve, f"bm{g}{pre}")])
        for tt in range(NTT):
            for li in range(2):
                g = tt * 2 + li
                P.op("dve", [], lambda e, tt=tt, li=li, g=g:
                     e.scalar_tensor_tensor(
                         t.slotf[:, g:g + 1], t.rnkb[:, li * 8 + tt:li * 8 + tt + 1],
                         1.0, t.basebuf[:, g:g + 1], ALU.mult, ALU.add),
                     (s_dve, f"slotf{pre}") if g == NTT * 2 - 1 else None,
                     fwaits=[(s_dve, f"bs{NTT*2-1}{pre}")] if g == 0 else ())
        P.op("dve", list(st.ga_prev) + list(st.scat_prev),
             lambda e: e.tensor_copy(t.sloti[:], t.slotf[:]),
             (s_dve, f"sloti{pre}"), fwaits=[(s_dve, f"slotf{pre}")])
        P.op("dve", [pos16_ev, CST], lambda e: e.tensor_scalar(
            t.cmpb[:], t.cb[0:16, CB_C128:CB_C128 + NCH], t.pos16[:, 0:1],
            None, ALU.is_ge), (s_dve, f"cmp{pre}"))
        cc = st.cc
        st.cc += 1
        ps = t.ps[cc % 3]
        P.op("pe", [(s_dve, f"cmp{pre}")] + pair_waits(cc),
             lambda e, ps=ps: e.matmul(ps[0:1, 32:64], t.ones16[:],
                                       t.cmpb[:], start=True, stop=True),
             (s_pe, f"pf{cc}"))
        P.op("dve", [(s_pe, f"pf{cc}")], lambda e, ps=ps: e.tensor_scalar(
            pr[0:1, 48:80], ps[0:1, 32:64], 128.0, -128.0, ALU.mult, ALU.add),
            (s_dve, f"pd{cc}"))
        st.pair_free[cc] = (s_dve, f"pd{cc}")
        cc = st.cc
        st.cc += 1
        ps = t.ps[cc % 3]
        P.op("pe", [(s_dve, f"pd{cc-1}")] + pair_waits(cc),
             lambda e, ps=ps: e.matmul(ps[:, 0:32], t.onesr[:],
                                       pr[0:1, 48:80], start=True, stop=True),
             (s_pe, f"pf{cc}"))
        P.op("dve", [(s_pe, f"pf{cc}"), CST, (s_zz, "zz")],
             lambda e, ps=ps: e.tensor_scalar(
                 t.idxwf[:], ps[:, 0:32], t.cb[:, CB_IOTA:CB_IOTA + 1], None,
                 ALU.add), (s_dve, f"pd{cc}"))
        st.pair_free[cc] = (s_dve, f"pd{cc}")
        P.op("dve", list(getattr(st, "wg_final", [])),
             lambda e: e.tensor_copy(t.idxw_i[:], t.idxwf[:]),
             (s_dve, f"idxw{b}p{p}"), fwaits=[(s_dve, f"pd{cc}")])
        for tt in range(NTT):
            for li in range(2):
                g = tt * 2 + li
                wsl = (t.rsc[:, tt * 4 + 2:tt * 4 + 3] if li == 0
                       else t.rsc[:, tt * 4 + 3:tt * 4 + 4])
                w = [(s_dve, f"sloti{pre}"), CST, (s_zz, "zz")]
                if g == 0:
                    w += list(st.wsl_prev)
                P.op("gp", w, lambda e, g=g, wsl=wsl: e.indirect_dma_start(
                    t.ws_d, IndirectOffsetOnAxis(t.sloti[:, g:g + 1], 0),
                    wsl, None),
                    (s_sc, f"sc{pre}g{g}"))
        for tt in range(NTT):
            for li in range(2):
                g = tt * 2 + li
                w = [(s_dve, f"sloti{pre}"), CST, (s_zz, "zza")]
                P.op("gp", w, lambda e, tt=tt, g=g: e.indirect_dma_start(
                    t.tokmap_d, IndirectOffsetOnAxis(t.sloti[:, g:g + 1], 0),
                    t.toki_s[:, tt:tt + 1], None),
                    (s_sc, f"tk{pre}g{g}"))
        st.scat_ev = (s_sc, f"tk{pre}g{NTT*2-1}")

    def shared_fills(k, pl, j, lnph, b):
        lastug = None
        for et in range(NET):
            for tc in range(T // 512):
                cc = st.cc
                st.cc += 1
                ps = t.ps[cc % 3]
                w = [st.pool_evu[pl],
                     (s_act, f"trc{b}p{lnph}t{tc * 4 + 3}")]
                w += pair_waits(cc)
                if k == NEXP - 1:
                    w.append(dnd(pl - 1))
                sgp = cc % 2

                def fillug(e, j=j, et=et, tc=tc, ps=ps):
                    last = None
                    for half in (0, 1):
                        for dt in range(NDT):
                            wcol = dt * 1024 + half * 512 + et * 128
                            last = e.matmul(
                                ps[:, half * 512:half * 512 + 512],
                                t.wugS[j][:, wcol:wcol + 128],
                                t.xlnT[:, dt * T + tc * 512: dt * T + (tc + 1) * 512],
                                start=(dt == 0), stop=(dt == NDT - 1))
                    return last
                P.op("pe", w, fillug, (s_pe, f"pf{cc}"))
                sw = [(s_pe, f"pf{cc}")]
                if sgp in st.sgfree:
                    sw.append(st.sgfree[sgp])
                P.op("act", sw, lambda e, ps=ps, sgp=sgp: e.activation(
                    t.sgb[:, sgp * 512:(sgp + 1) * 512], ps[:, 512:1024],
                    AF.Silu), (s_act, f"sil{cc}"))
                hcol = et * T + tc * 512
                P.op("dve", [(s_pe, f"pf{cc}"), (s_act, f"sil{cc}")],
                     lambda e, ps=ps, sgp=sgp, hcol=hcol: e.tensor_mul(
                         t.hidden[:, hcol:hcol + 512], ps[:, 0:512],
                         t.sgb[:, sgp * 512:(sgp + 1) * 512]),
                     (s_dve, f"pd{cc}"))
                st.pair_free[cc] = (s_dve, f"pd{cc}")
                st.sgfree[sgp] = (s_dve, f"pd{cc}")
                lastug = cc
        P.alias(s_pe, f"ugd_u{pl}", f"pf{lastug}")
        return (s_dve, f"pd{lastug}")

    def shared_downs(k, pl, j, hid_done, b, p, xp):
        for tt in range(NTT):
            cc = st.cc
            st.cc += 1
            ps = t.ps[cc % 3]
            w = [hid_done, st.pool_evd[pl]] + pair_waits(cc)

            def filldn(e, j=j, tt=tt, ps=ps):
                last = None
                for fc in range(2):
                    for et in range(NET):
                        last = e.matmul(
                            ps[:, fc * 512:(fc + 1) * 512],
                            t.hidden[:, et * T + tt * 128: et * T + (tt + 1) * 128],
                            t.wdnS[j][:, et * 1024 + fc * 512: et * 1024 + (fc + 1) * 512],
                            start=(et == 0), stop=(et == NET - 1))
                return last
            P.op("pe", w, filldn, (s_pe, f"pf{cc}"))
            if tt == NTT - 1:
                P.alias(s_pe, f"dnd_u{pl}", f"pf{cc}")
            xt = t.xbuf[xp][:, tt * D:(tt + 1) * D]
            P.op("dve", [(s_pe, f"pf{cc}")],
                 lambda e, ps=ps, xt=xt: e.scalar_tensor_tensor(
                     xt, ps[:], 0.5, xt, ALU.mult, ALU.add),
                 (s_dve, f"pd{cc}"))
            st.pair_free[cc] = (s_dve, f"pd{cc}")
            P.alias(s_dve, f"shd{b}p{p}k{k}tt{tt}", f"pd{cc}")

    def sparse_chunks(b, p, xp, lnph):
        pre = f"{b}p{p}"
        idx_ev = (s_dve, f"idxw{b}p{p}")
        pl0 = st.pl
        st.pl += NCH
        pls = list(range(pl0, pl0 + NCH))

        xl_done = [(s_xl[0], f"xl{b}p{lnph}t{NTT-2}"),
                   (s_xl[1], f"xl{b}p{lnph}t{NTT-1}")]

        def emit_wsc(c):
            cp = c % 2
            tw = [st.scat_ev]
            if c >= 2:
                tw.append((s_dve, f"drn{pre}c{c-2}"))
            P.op("sp", tw, lambda e, c=c, cp=cp: e.dma_start(
                out=t.tokmap_i[cp][:],
                in_=t.tokmap_d[c * 128:(c + 1) * 128, :]),
                (s_tw[cp], f"twa{pre}c{c}"))
            P.op("sp", [], lambda e, c=c, cp=cp: e.dma_start(
                out=t.wsc[cp][:], in_=t.ws_d[c * 128:(c + 1) * 128, :]),
                (s_tw[cp], f"tw{pre}c{c}"))

        def emit_xg(c):
            cp = c % 2
            w = [(s_tw[cp], f"tw{pre}c{c}")] + xl_done
            if c >= 2:
                w.append((s_pe, f"ctr{pre}c{c-2}"))
            elif st.ctr_prev is not None:
                w.append(st.ctr_prev)
            P.op("gp", w, lambda e, c=c, cp=cp: e.indirect_dma_start(
                t.xc[cp][:], None, t.xlnb_d,
                IndirectOffsetOnAxis(t.tokmap_i[cp][:, 0:1], 0)),
                (s_xg[cp], f"xg{pre}c{c}"))

        def compute(c):
            pl = pls[c]
            ju, jd = pl % 3, pl % 2
            cp = c % 2
            ptr = t.psTR if cp == 0 else t.psTR2
            trw = [(s_xg[cp], f"xg{pre}c{c}"),
                   (s_act, f"trc{b}p{lnph}t{NTT-1}"), CST]
            if c >= 2:
                trw.append((s_act, f"ctc{pre}c{c-2}"))

            def ctr(e, cp=cp, ptr=ptr):
                last = None
                for dt in range(NDT):
                    last = e.transpose(ptr[:, dt * 128:(dt + 1) * 128],
                                       t.xc[cp][:, dt * 128:(dt + 1) * 128],
                                       t.identb[:])
                return last
            P.op("pe", trw, ctr, (s_pe, f"ctr{pre}c{c}"))
            P.op("act", [(s_pe, f"ctr{pre}c{c}")],
                 lambda e, cp=cp, ptr=ptr: e.copy(
                     t.xcT[cp].ap().rearrange("p (dt c) -> p dt c", c=128),
                     ptr.ap().rearrange("p (dt c) -> p dt c", c=128)),
                 (s_act, f"ctc{pre}c{c}"))
            cc = st.cc
            st.cc += 1
            ps = t.ps[cc % 3]
            sgp = cc % 2
            w = [st.pool_evu[pl], (s_act, f"ctc{pre}c{c}")] + pair_waits(cc)

            def cfill_h(half):
                def f(e, ju=ju, cp=cp, ps=ps, half=half):
                    last = None
                    for et in range(NET):
                        for dt in range(NDT):
                            wcol = dt * 1024 + half * 512 + et * 128
                            last = e.matmul(
                                ps[:, half * 512 + et * 128: half * 512 + (et + 1) * 128],
                                t.wugS[ju][:, wcol:wcol + 128],
                                t.xcT[cp][:, dt * 128:(dt + 1) * 128],
                                start=(dt == 0), stop=(dt == NDT - 1))
                    return last
                return f
            P.op("pe", w, cfill_h(1), (s_pe, f"cfg{cc}"))
            P.op("pe", [], cfill_h(0), (s_pe, f"pf{cc}"))
            P.alias(s_pe, f"ugd_u{pl}", f"pf{cc}")
            sw = [(s_pe, f"cfg{cc}")]
            if sgp in st.sgfree:
                sw.append(st.sgfree[sgp])
            P.op("act", sw, lambda e, ps=ps, sgp=sgp: e.activation(
                t.sgb[:, sgp * 512:(sgp + 1) * 512], ps[:, 512:1024], AF.Silu),
                (s_act, f"sil{cc}"))
            dw = [(s_pe, f"pf{cc}"), (s_act, f"sil{cc}")]
            if cp in st.hcfree:
                dw.append(st.hcfree[cp])
            P.op("dve", dw, lambda e, ps=ps, sgp=sgp, cp=cp: e.tensor_mul(
                t.hc[cp][:], ps[:, 0:512],
                t.sgb[:, sgp * 512:(sgp + 1) * 512]), (s_dve, f"pd{cc}"))
            st.pair_free[cc] = (s_dve, f"pd{cc}")
            st.sgfree[sgp] = (s_dve, f"pd{cc}")
            hc_ready = (s_dve, f"pd{cc}")
            cc2 = st.cc
            st.cc += 1
            ps2 = t.ps[cc2 % 3]
            w = [hc_ready, st.pool_evd[pl]] + pair_waits(cc2)

            def cdown(e, jd=jd, cp=cp, ps2=ps2):
                last = None
                for fc in range(2):
                    for et in range(NET):
                        last = e.matmul(
                            ps2[:, fc * 512:(fc + 1) * 512],
                            t.hc[cp][:, et * 128:(et + 1) * 128],
                            t.wdnS[jd][:, et * 1024 + fc * 512: et * 1024 + (fc + 1) * 512],
                            start=(et == 0), stop=(et == NET - 1))
                return last
            P.op("pe", w, cdown, (s_pe, f"pf{cc2}"))
            P.alias(s_pe, f"dnd_u{pl}", f"pf{cc2}")
            st.hcfree[cp] = (s_pe, f"pf{cc2}")
            ow = [(s_pe, f"pf{cc2}"), (s_tw[cp], f"tw{pre}c{c}")]
            if c == 0:
                ow += list(st.ga_prev)
            if st.os_store.get(cp) is not None:
                ow.append(st.os_store[cp])
            ost = t.ostY[:, cp * 1024:(cp + 1) * 1024]
            P.op("dve", ow, lambda e, ps2=ps2, ost=ost, cp=cp: e.tensor_scalar(
                ost, ps2[:], t.wsc[cp][:, 0:1], None, ALU.mult),
                (s_dve, f"pd{cc2}"))
            P.alias(s_dve, f"drn{pre}c{c}", f"pd{cc2}")
            st.pair_free[cc2] = (s_dve, f"pd{cc2}")
            P.op("sp", [(s_dve, f"pd{cc2}")],
                 lambda e, c=c, ost=ost: e.dma_start(
                     out=t.out_d[c * 128:(c + 1) * 128, :], in_=ost),
                 (s_os[cp], f"os{pre}c{c}"))
            st.os_store[cp] = (s_os[cp], f"os{pre}c{c}")

        emit_wsc(0)
        emit_wsc(1)
        emit_xg(0)
        emit_xg(1)
        chunk_load_wug(pls[0], 0, idx_ev)
        chunk_load_wdn(pls[0], 0, idx_ev)
        chunk_load_wug(pls[1], 1, idx_ev)
        chunk_load_wdn(pls[1], 1, idx_ev)
        chunk_load_wug(pls[2], 2, idx_ev)
        for c in range(NCH):
            if c + 2 < NCH:
                emit_wsc(c + 2)
                emit_xg(c + 2)
            if c + 3 < NCH:
                chunk_load_wug(pls[c + 3], c + 3, idx_ev)
            if c + 2 < NCH:
                chunk_load_wdn(pls[c + 2], c + 2, idx_ev)
            compute(c)
        st.os_final = [st.os_store[0], st.os_store[1]]
        st.wsl_prev = [(s_tw[0], f"tw{pre}c{NCH-2}"),
                       (s_tw[1], f"tw{pre}c{NCH-1}")]
        st.xg_prev = [(s_xg[0], f"xg{pre}c{NCH-2}"),
                      (s_xg[1], f"xg{pre}c{NCH-1}")]
        st.scat_prev = [st.scat_ev]
        st.ctr_prev = (s_pe, f"ctr{pre}c{NCH-1}")
        st.wg_final = [st.pool_evu[pls[-1]], st.pool_evu[pls[-2]],
                       st.pool_evu[pls[-3]],
                       st.pool_evd[pls[-1]], st.pool_evd[pls[-2]]]

    def gathers(b, p, xp, tail_s1=None):
        pre = f"{b}p{p}"
        lnph = 0 if p == 0 else 2
        for tt in range(NTT):
            w = list(st.os_final)
            w.append((s_dve, f"shd{b}p{p}k{NEXP-1}tt{tt}"))
            w.append((s_act, f"lnn{b}p{lnph}t{tt}"))
            w.append((s_dve, f"lns{b}p{lnph}t{tt}"))
            xt = t.xbuf[xp][:, tt * D:(tt + 1) * D]
            P.op("gp", w, lambda e, tt=tt, xt=xt: e.indirect_dma_start(
                xt, None, t.out_d,
                IndirectOffsetOnAxis(t.sloti[:, tt * 2:tt * 2 + 1], 0),
                compute_op=ALU.add),
                (s_ga[tt], f"ga1{pre}t{tt}"))
        for tt in range(NTT):
            xt = t.xbuf[xp][:, tt * D:(tt + 1) * D]
            P.op("gp", [(s_ga[tt], f"ga1{pre}t{tt}")],
                 lambda e, tt=tt, xt=xt: e.indirect_dma_start(
                     xt, None, t.out_d,
                     IndirectOffsetOnAxis(t.sloti[:, tt * 2 + 1:tt * 2 + 2], 0),
                     compute_op=ALU.add),
                 (s_ga[tt], f"ga2{pre}t{tt}"))
            st.galast[(b, p, tt)] = f"ga2{pre}t{tt}"
            if tail_s1 is not None:
                tail_s1(tt)
        st.ga_prev = [(s_ga[tt], f"ga2{pre}t{tt}") for tt in range(NTT)]

    def ffn_pass(b, p, xp, tail_s1=None, tail_s24=None):
        lnph = 0 if p == 0 else 2
        pl16, ju16, jd16 = pool_load_regular(16)
        pl17, ju17, jd17 = pool_load_regular(17)
        hd16 = shared_fills(16, pl16, ju16, lnph, b)
        shared_downs(16, pl16, jd16, hd16, b, p, xp)
        hd17 = shared_fills(NEXP - 1, pl17, ju17, lnph, b)
        shared_downs(17, pl17, jd17, hd17, b, p, xp)
        sparse_chunks(b, p, xp, lnph)
        gathers(b, p, xp, tail_s1=tail_s1)
        if tail_s24 is not None:
            tail_s24()

    def ssm_phase(b, xp):
        ln_rest(b, 1, xp)
        plw, jw, _jdw = pool_load_regular(NEXP, wdn=False)
        ccZ = st.cc
        st.cc += 1
        psZ = t.ps[ccZ % 3]
        for tt in range(NTT):
            w = [(s_act, f"trc{b}p1t{tt}"), GW]
            if tt == 0:
                w += pair_waits(ccZ)

            def lbops(e, tt=tt, psZ=psZ):
                last = None
                for dt in range(NDT):
                    last = e.matmul(psZ[:, tt * 32:(tt + 1) * 32],
                                    t.xlnT[:, dt * T + tt * 128: dt * T + (tt + 1) * 128],
                                    t.lb_s[:, dt * 32:(dt + 1) * 32],
                                    start=(dt == 0), stop=(dt == NDT - 1))
                return last
            P.op("pe", w, lbops, (s_pe, f"pf{ccZ}") if tt == NTT - 1 else None)
        def zdrain(e, psZ=psZ):
            last = None
            for tt in range(NTT):
                last = e.tensor_add(t.zlb[:, tt * 32:(tt + 1) * 32],
                                    psZ[:, tt * 32:(tt + 1) * 32],
                                    t.cb[:, CB_BL:CB_BL + 32])
            return last
        P.op("dve", [(s_pe, f"pf{ccZ}"), CST], zdrain, (s_dve, f"pd{ccZ}"))
        st.pair_free[ccZ] = (s_dve, f"pd{ccZ}")
        ccL = st.cc
        st.cc += 1
        psL = t.ps[ccL % 3]

        def latr(e, psL=psL):
            last = None
            for tt in range(NTT):
                last = e.transpose(psL[0:16, tt * 128:(tt + 1) * 128],
                                   t.zlb[:, tt * 32:tt * 32 + 16], t.identf[:])
            return last
        P.op("pe", [(s_dve, f"pd{ccZ}"), CST] + pair_waits(ccL),
             latr, (s_pe, f"pf{ccL}"))
        P.op("act", [(s_pe, f"pf{ccL}")],
             lambda e, psL=psL: e.copy(t.zlamT, psL[0:16, 0:T]),
             (s_act, f"pd{ccL}"))
        st.pair_free[ccL] = (s_act, f"pd{ccL}")
        ccB = st.cc
        st.cc += 1
        psB = t.ps[ccB % 3]

        def bxtr(e, psB=psB):
            last = None
            for tt in range(NTT):
                last = e.transpose(psB[0:16, tt * 128:(tt + 1) * 128],
                                   t.zlb[:, tt * 32 + 16:tt * 32 + 32], t.identf[:])
            return last
        P.op("pe", pair_waits(ccB), bxtr, (s_pe, f"pf{ccB}"))
        P.op("act", [(s_pe, f"pf{ccB}")],
             lambda e, psB=psB: e.copy(t.zbxT[:], psB[0:16, 0:T]),
             (s_act, f"pd{ccB}"))
        st.pair_free[ccB] = (s_act, f"pd{ccB}")
        P.op("act", [], lambda e: e.activation(t.zlamT, t.zlamT, AF.Sigmoid),
             (s_act, f"sig{b}"), fwaits=[(s_act, f"pd{ccL}")])
        P.op("dve", [(s_act, f"sig{b}"), (s_act, f"pd{ccB}")],
             lambda e: e.tensor_tensor_scan(t.rT[:], t.zlamT, t.zbxT[:],
                                            t.hcarry[:, 0:1], ALU.mult, ALU.add),
             (s_dve, f"scan{b}"))

        def hscopy(e):
            e.tensor_copy(t.hcarry[:, 0:1], t.rT[:, T - 1:T])
            return e.tensor_copy(t.hs_bf[:], t.rT[:])
        P.op("dve", [], hscopy, (s_dve, f"hsbf{b}"), fwaits=[(s_dve, f"scan{b}")])

        for g in range(NTT // 2):
            tts = range(g * 2, g * 2 + 2)
            for tt in tts:
                ccg = st.cc
                st.cc += 1
                psg = t.ps[ccg % 3]
                w = [st.pool_evu[plw], (s_act, f"trc{b}p1t{tt}")] + pair_waits(ccg)

                def gops(e, tt=tt, psg=psg, jw=jw):
                    last = None
                    for fc in range(2):
                        for dt in range(NDT):
                            last = e.matmul(
                                psg[:, fc * 512:(fc + 1) * 512],
                                t.xlnT[:, dt * T + tt * 128: dt * T + (tt + 1) * 128],
                                t.wugS[jw][:, dt * 1024 + fc * 512: dt * 1024 + (fc + 1) * 512],
                                start=(dt == 0), stop=(dt == NDT - 1))
                    return last
                P.op("pe", w, gops, (s_pe, f"pf{ccg}"))
                if tt == NTT - 1:
                    P.alias(s_pe, f"ugd_u{plw}", f"pf{ccg}")
                    P.alias(s_pe, f"dnd_u{plw}", f"pf{ccg}")
                ccy = st.cc
                st.cc += 1
                psy = t.ps[ccy % 3]
                w = [(s_dve, f"hsbf{b}"), GW] + pair_waits(ccy)

                def yops(e, tt=tt, psy=psy):
                    last = None
                    for fc in range(2):
                        last = e.matmul(psy[:, fc * 512:(fc + 1) * 512],
                                        t.hs_bf[:, tt * 128:(tt + 1) * 128],
                                        t.wct_s[:, fc * 512:(fc + 1) * 512],
                                        start=True, stop=True)
                    return last
                P.op("pe", w, yops, (s_pe, f"pf{ccy}"))
                yp = tt % 2
                gw_ = [(s_pe, f"pf{ccg}")]
                if tt >= 1:
                    gw_.append((s_dve, f"ymul{b}t{tt-1}"))
                P.op("act", gw_, lambda e, psg=psg: e.activation(
                    t.sgb[:], psg[:], AF.Sigmoid), (s_act, f"gs{b}t{tt}"))
                st.pair_free[ccg] = (s_act, f"gs{b}t{tt}")
                yt = t.ostY[:, yp * 1024:(yp + 1) * 1024]
                yw = [(s_pe, f"pf{ccy}"), (s_act, f"gs{b}t{tt}")]
                if tt >= 2:
                    yw.append((s_dve, f"sadd{b}tt{tt-2}"))
                P.op("dve", yw, lambda e, psy=psy, yt=yt: e.tensor_mul(
                    yt, psy[:], t.sgb[:]), (s_dve, f"ymul{b}t{tt}"))
                st.pair_free[ccy] = (s_dve, f"ymul{b}t{tt}")
                dmp = [st.dump_prev[0]] if st.dump_prev[0] is not None else []
                P.op("act", [(s_dve, f"ymul{b}t{tt}")],
                     lambda e, yt=yt, tt=tt: e.activation(
                         t.dump[:], yt, AF.Square,
                         accum_out=t.sstat[:, tt * 2:tt * 2 + 1]),
                     (s_act, f"sq2{b}t{tt}"), fwaits=dmp)
                st.dump_prev[0] = (s_act, f"sq2{b}t{tt}")
            if g == NTT // 2 - 1:
                for sgp_ in range(2):
                    st.sgfree[sgp_] = (s_dve, f"ymul{b}t{NTT-1}")
            for tt in tts:
                P.op("act", [CST], lambda e, tt=tt: e.activation(
                    t.sstat[:, tt * 2 + 1:tt * 2 + 2],
                    t.sstat[:, tt * 2:tt * 2 + 1], AF.Ln,
                    bias=t.cb[:, CB_EPS_RMS:CB_EPS_RMS + 1],
                    scale=t.cb[:, CB_INVD:CB_INVD + 1]),
                    (s_act, f"r2a{b}t{tt}"), fwaits=[(s_act, f"sq2{b}t{tt}")])
            for tt in tts:
                P.op("act", [], lambda e, tt=tt: e.activation(
                    t.sstat[:, tt * 2 + 1:tt * 2 + 2],
                    t.sstat[:, tt * 2 + 1:tt * 2 + 2], AF.Exp,
                    scale=t.cb[:, CB_MHALF:CB_MHALF + 1]),
                    (s_act, f"rs2{b}t{tt}"), fwaits=[(s_act, f"r2a{b}t{tt}")])
            for tt in tts:
                def tsbop(e, tt=tt):
                    yp = tt % 2
                    ss = t.sstat[:, tt * 2:tt * 2 + 2]
                    yt = t.ostY[:, yp * 1024:(yp + 1) * 1024]
                    return e.scalar_tensor_tensor(yt, yt, ss[:, 1:2],
                                                  t.cb[:, CB_RW:CB_RW + D],
                                                  ALU.mult, ALU.mult)
                tfw = [(s_dve, f"sadd{b}tt{tt-1}")] if tt > 0 else []
                P.op("dve", [(s_act, f"rs2{b}t{tt}"), CST], tsbop,
                     (s_dve, f"tsb{b}t{tt}"), fwaits=tfw)

                def saddop(e, tt=tt):
                    yp = tt % 2
                    yt = t.ostY[:, yp * 1024:(yp + 1) * 1024]
                    return e.tensor_add(
                        t.xbuf[xp][:, tt * D:(tt + 1) * D],
                        t.xbuf[xp][:, tt * D:(tt + 1) * D], yt)
                P.op("dve", [], saddop,
                    (s_dve, f"sadd{b}tt{tt}"), fwaits=[(s_dve, f"tsb{b}t{tt}")])
                ln_s1(b, 2, xp, tt)
        ln_s24(b, 2)

    xload(0)
    for b in range(NB):
        xp = b % 2
        if b == 0:
            for tt in range(NTT):
                ln_s1(0, 0, 0, tt)
            ln_s24(0, 0)
        ln_rest(b, 0, xp)
        ffn_pass(b, 0, xp,
                 tail_s1=lambda tt, b=b, xp=xp: ln_s1(b, 1, xp, tt),
                 tail_s24=lambda b=b: ln_s24(b, 1))
        ssm_phase(b, xp)
        ln_rest(b, 2, xp)
        if b + 1 < NB:
            xload(b + 1)
            ffn_pass(b, 1, xp,
                     tail_s1=lambda tt, b=b: ln_s1(b + 1, 0, (b + 1) % 2, tt),
                     tail_s24=lambda b=b: ln_s24(b + 1, 0))
        else:
            ffn_pass(b, 1, xp)
        stw = [(s_ga[tt], st.galast[(b, 1, tt)]) for tt in range(NTT)]
        P.op("sp", stw,
             lambda e, b=b, xp=xp: e.dma_start(
                 out=t.y_d[b * T:(b + 1) * T, :].rearrange("(tt p) d -> p tt d", p=128),
                 in_=t.xbuf[xp].ap().rearrange("p (tt d) -> p tt d", d=D)),
             (s_st, f"y{b}"))
    P.op("sp", [(s_st, f"y{NB-1}")], lambda e: e.nop(), None)


def build_program():
    nc = bass.Bass("TRN2", target_bir_lowering=False, debug=False,
                   detect_race_conditions=False)
    t = build_tensors(nc)
    emit(Px(None), t)
    with nc.Block() as blk:
        @blk.sync
        def _(eng):
            emit(Px("sp", eng), t)

        @blk.gpsimd
        def _(eng):
            emit(Px("gp", eng), t)

        @blk.tensor
        def _(eng):
            emit(Px("pe", eng), t)

        @blk.vector
        def _(eng):
            emit(Px("dve", eng), t)

        @blk.scalar
        def _(eng):
            emit(Px("act", eng), t)
    return nc


def build_cbias(router_bias, b_lambda, rms_weight):
    f32 = np.float32
    cb = np.zeros((128, CB_W), dtype=f32)
    cb[:, CB_RB:CB_RB + 16] = np.asarray(router_bias, f32)[None, :]
    cb[:, CB_BL:CB_BL + 16] = np.asarray(b_lambda, f32)[None, :]
    cb[:, CB_RW:CB_RW + D] = np.asarray(rms_weight, f32)[None, :]
    cb[:, CB_EPS_LN] = 1e-5
    cb[:, CB_EPS_RMS] = 1e-6
    cb[:, CB_MHALF] = -0.5
    cb[:, CB_INVD] = 1.0 / D
    cb[:, CB_IOTA] = np.arange(128, dtype=f32)
    cb[:, CB_C128:CB_C128 + NCH] = 128.0 * np.arange(NCH, dtype=f32)[None, :]
    cb[:, CB_ONES16:CB_ONES16 + 16] = 1.0
    return cb


def kernel(L0_attn_x, L0_attn_W_lambda, L0_attn_b_lambda, L0_attn_W_B,
           L0_attn_W_C, L0_attn_W_gate, L0_attn_rms_weight,
           L0_ffn_router_w, L0_ffn_router_bias, L0_ffn_up_proj,
           L0_ffn_gate_proj, L0_ffn_down_proj, L0_ffn_sh_gate,
           L0_ffn_sh_up, L0_ffn_sh_down):
    import ml_dtypes
    f32 = np.float32
    x = np.ascontiguousarray(np.asarray(L0_attn_x, dtype=f32))
    cb = build_cbias(L0_ffn_router_bias, L0_attn_b_lambda, L0_attn_rms_weight)
    toki = (np.arange(NTT, dtype=np.int32)[None, :] * 128
            + np.arange(128, dtype=np.int32)[:, None]).astype(np.int32)
    common = {
        "up": np.ascontiguousarray(np.asarray(L0_ffn_up_proj, f32)),
        "gate": np.ascontiguousarray(np.asarray(L0_ffn_gate_proj, f32)),
        "down": np.ascontiguousarray(np.asarray(L0_ffn_down_proj, f32)),
        "shu_t": np.ascontiguousarray(np.asarray(L0_ffn_sh_up, f32).T),
        "shg_t": np.ascontiguousarray(np.asarray(L0_ffn_sh_gate, f32).T),
        "shd_t": np.ascontiguousarray(np.asarray(L0_ffn_sh_down, f32).T),
        "wgate_t": np.ascontiguousarray(np.asarray(L0_attn_W_gate, f32).T),
        "router_t": np.ascontiguousarray(np.asarray(L0_ffn_router_w, f32).T),
        "wlamb_t": np.ascontiguousarray(np.concatenate(
            [np.asarray(L0_attn_W_lambda, f32).T,
             np.asarray(L0_attn_W_B, f32).T], axis=1)),
        "wct": np.ascontiguousarray(np.asarray(L0_attn_W_C, f32).T),
        "cbias": cb,
        "identb": np.eye(128, dtype=ml_dtypes.bfloat16),
        "identf": np.eye(128, dtype=f32),
        "toki": np.ascontiguousarray(toki),
    }
    in_maps = [dict(common, x=np.ascontiguousarray(x[c])) for c in range(8)]
    nc = build_program()
    res = run_bass_kernel_spmd(nc, in_maps, core_ids=list(range(8)))
    out = np.stack([np.asarray(res.results[c]["y"]) for c in range(8)], axis=0)
    return out.astype(np.float32, copy=False)
